# version 26
# baseline (speedup 1.0000x reference)
import numpy as np
from contextlib import ExitStack

import concourse.bass as bass
import concourse.mybir as mybir
from concourse.bass import IndirectOffsetOnAxis
from concourse.bass_utils import run_bass_kernel_spmd

F32 = mybir.dt.float32
BF16 = mybir.dt.bfloat16
U32 = mybir.dt.uint32
AF = mybir.ActivationFunctionType
ALU = mybir.AluOpType
AX = mybir.AxisListType.X

NCORES = 8
TSEQ = 2048
NSEQ = 4
TC = NSEQ * TSEQ
D = 1024
DCH = 8
E = 32
H = 512
HK = 4096
HKCH = 32
M = 1280
NT = TC // 128
NB = 4
EPS = 1e-5
LN_HALF = float(np.log(0.5))

ENGINES = ("sync", "vector", "scalar", "tensor", "gpsimd")


class Sem:
    __slots__ = ("h", "n")

    def __init__(self, h):
        self.h = h
        self.n = 0

    def cur(self):
        return (self, self.n)


def _merge_waits(waits):
    best = {}
    for w in waits:
        if w is None:
            continue
        s, v = w
        if v <= 0:
            continue
        if best.get(s.h, -1) < v:
            best[s.h] = v
    return list(best.items())


class Ph:
    def __init__(self):
        self.q = {e: [] for e in ENGINES}

    def flush(self, block):
        for e in ENGINES:
            fns = self.q[e]
            if fns:
                def runner(X, fns=fns):
                    for f in fns:
                        f(X)
                getattr(block, e)(runner)


def build(upto=99, debug=False):
    nc = bass.Bass("TRN2", target_bir_lowering=False)

    def din(name, shape, dtype=F32):
        return nc.dram_tensor(name, shape, dtype, kind="ExternalInput").ap()

    x_in = din("x", [TC, D])
    ln1w = din("ln1w", [1, D]);  ln1b = din("ln1b", [1, D])
    ln2w = din("ln2w", [1, D]);  ln2b = din("ln2b", [1, D])
    WrT = din("WrT", [D, D]);  WkT = din("WkT", [D, D])
    WvT = din("WvT", [D, D]);  WoT = din("WoT", [D, D])
    WfkT = din("WfkT", [D, HK]); WfvT = din("WfvT", [HK, D]); WfrT = din("WfrT", [D, D])
    Wg = din("Wg", [D, E])
    Ws1 = din("Ws1", [D, D]); Ws3 = din("Ws3", [D, D]); Ws2 = din("Ws2", [D, D])
    W1 = din("W1", [E, D, H]); W3 = din("W3", [E, D, H]); W2 = din("W2", [E, H, D])
    decay = din("decay", [128, DCH]); u_in = din("u", [128, DCH])
    tri_in = din("tri", [128, 128])
    onesc = din("onesc", [128, 1]); onesr = din("onesr", [1, 128])
    iotaEM_in = din("iotaEM_i", [128, E]); iotap_in = din("iotap_i", [128, 1])

    sk = "ExternalOutput" if debug else "Internal"

    def dsc(name, shape, dtype=F32):
        return nc.dram_tensor(name, shape, dtype, kind=sk).ap()

    xn_f = dsc("xn_f", [TC, D])
    xn_bf = dsc("xn_bf", [TC, D], BF16)
    xna_bf = dsc("xna_bf", [TC, D], BF16)
    k_f = dsc("k_f", [DCH, 128, TC])
    v_f = dsc("v_f", [DCH, 128, TC])
    r_f = dsc("r_f", [DCH, 128, TC], BF16)
    rw_bf = dsc("rw_bf", [DCH, 128, TC], BF16)
    xg = dsc("xg_s", [E * M, D], BF16)
    idw = dsc("idw_s", [E * M, 2])
    y_f = dsc("y_f", [TC + 128, D])
    a2_f = dsc("a2_f", [TC, D])
    xn2_bf = dsc("xn2_bf", [TC, D], BF16)
    kk_bf = dsc("kk_bf", [HKCH, 128, TC], BF16)
    out_slots = dsc("oslot", [E * M, D], BF16)
    slots_tok = dsc("stok", [TC, 4], U32)
    a2xy_f = dsc("a2xy_f", [TC, D])
    w1_bf = dsc("w1_bf", [E, 128, DCH, H], BF16)
    w3_bf = dsc("w3_bf", [E, 128, DCH, H], BF16)
    w2_bf = dsc("w2_bf", [E, 128, 4, D], BF16)
    out_d = nc.dram_tensor("out", [TC, D], F32, kind="ExternalOutput").ap()

    with ExitStack() as ctx0:
        def sbt(name, shape, dtype=F32, stack=None):
            return (stack or ctx0).enter_context(nc.sbuf_tensor(name, shape, dtype))

        d_ex = sbt("d_ex", [128, DCH]); eu = sbt("eu", [128, DCH])
        tmp8 = sbt("tmp8", [128, DCH])
        tri_f = sbt("tri_f", [128, 128])
        ones_c = sbt("ones_c", [128, 1]); ones_r = sbt("ones_r", [1, 128])
        iotaEM = sbt("iotaEM", [128, E]); clampM = sbt("clampM", [128, E])
        iota_p = sbt("iota_p", [128, 1])
        eps_c = sbt("eps_c", [128, 1]); lnh_c = sbt("lnh_c", [128, 1])

        pb = [ctx0.enter_context(nc.psum_tensor(f"pb{i}", [128, 512], F32))
              for i in range(8)]

        q_s = Sem(ctx0.enter_context(nc.semaphore("q_s")))
        q_g = Sem(ctx0.enter_context(nc.semaphore("q_g")))
        s_v = Sem(ctx0.enter_context(nc.semaphore("s_v")))
        s_a = Sem(ctx0.enter_context(nc.semaphore("s_a")))
        s_p = Sem(ctx0.enter_context(nc.semaphore("s_p")))
        q_a = Sem(ctx0.enter_context(nc.semaphore("q_a")))
        s_g = Sem(ctx0.enter_context(nc.semaphore("s_g")))
        allsems = [q_s, q_g, s_v, s_a, s_p, q_a, s_g]

        def emit(ph, eng, build_fn, waits=(), inc=None, inck=1):
            wl = _merge_waits(waits)
            if inc is not None:
                inc.n += inck
                ih = inc.h
            else:
                ih = None

            def fn(X, wl=wl, build_fn=build_fn, ih=ih, inck=inck):
                for h, v in wl:
                    X.wait_ge(h, v)
                inst = build_fn(X)
                if ih is not None:
                    inst.then_inc(ih, inck)
            ph.q[eng].append(fn)

        def vop(ph, bf_, *extra):
            emit(ph, "vector", bf_, waits=[s_v.cur(), *extra], inc=s_v)
            return s_v.cur()

        def aop(ph, bf_, *extra):
            emit(ph, "scalar", bf_, waits=[s_a.cur(), *extra], inc=s_a)
            return s_a.cur()

        def top(ph, bf_, *extra):
            emit(ph, "tensor", bf_, waits=list(extra), inc=s_p)
            return s_p.cur()

        def qop(ph, bf_, *extra):
            emit(ph, "sync", bf_, waits=list(extra), inc=q_s, inck=16)
            return q_s.cur()

        def gop(ph, bf_, *extra):
            emit(ph, "gpsimd", bf_, waits=list(extra), inc=q_g, inck=16)
            return q_g.cur()

        def sqop(ph, bf_, *extra):
            emit(ph, "scalar", bf_, waits=list(extra), inc=q_a, inck=16)
            return q_a.cur()

        def kop(ph, bf_, *extra):
            emit(ph, "gpsimd", bf_, waits=[s_g.cur(), *extra], inc=s_g)
            return s_g.cur()

        def allcur():
            return [s.cur() for s in allsems]

        class Precast:
            def __init__(self, stage, src_w, dst_w, rrs):
                self.stage = stage
                self.src_w = src_w
                self.dst_w = dst_w
                self.rrs = rrs
                self.i = 0
                self.stok = [None, None]

            def step(self, ph):
                if self.i >= E:
                    return
                e = self.i
                pp = e % 2
                lt = gop(ph, lambda g, e=e, pp=pp, s=self:
                         g.dma_start(out=s.stage[pp][:],
                                     in_=s.src_w[e].rearrange(s.rrs, p=128)),
                         self.stok[pp])
                self.stok[pp] = sqop(ph, lambda q, e=e, pp=pp, s=self:
                                     q.dma_start(out=s.dst_w[e],
                                                 in_=s.stage[pp][:]), lt)
                self.i += 1

            def finish(self, ph):
                while self.i < E:
                    self.step(ph)

        def run(ph, name="ph"):
            with nc.named_scope(name), nc.Block() as block:
                ph.flush(block)
                waits = _merge_waits(allcur())
                for e in ENGINES:
                    def bar(X, waits=waits):
                        for h, v in waits:
                            X.wait_ge(h, v)
                    getattr(block, e)(bar)

        prep_stack = ExitStack()
        sent = sbt("sent", [128, (E * M * 2) // 128], F32, prep_stack)
        ph = Ph()
        for dst, src in [(d_ex, decay), (eu, u_in), (tri_f, tri_in),
                         (ones_c, onesc), (ones_r, onesr),
                         (iotaEM, iotaEM_in), (iota_p, iotap_in)]:
            qop(ph, lambda q, dst=dst, src=src: q.dma_start(out=dst[:], in_=src))
        ldp = q_s.cur()
        aop(ph, lambda a: a.activation(tmp8[:], d_ex[:], AF.Exp), ldp)
        aop(ph, lambda a: a.activation(d_ex[:], tmp8[:], AF.Exp, scale=-1.0))
        t_eu = aop(ph, lambda a: a.activation(tmp8[:], eu[:], AF.Exp))
        vop(ph, lambda v: v.tensor_copy(eu[:], tmp8[:]), t_eu)
        vop(ph, lambda v: v.tensor_scalar_add(clampM[:], iotaEM[:], float(M - 1)), ldp)
        vop(ph, lambda v: v.memset(eps_c[:], EPS))
        vop(ph, lambda v: v.memset(lnh_c[:], LN_HALF))
        t_sent = vop(ph, lambda v: v.memset(sent[:], float(TC)))
        qop(ph, lambda q: q.dma_start(
            out=idw.rearrange("(p a) w -> p (a w)", p=128), in_=sent[:]), t_sent)
        run(ph, "prep")
        prep_stack.close()

        a_store_q = None
        a_store_g = None
        if upto >= 1:
            with ExitStack() as c1:
                w1bc = sbt("w1bc", [128, D], F32, c1)
                b1bc = sbt("b1bc", [128, D], F32, c1)
                pcs = [sbt(f"A_pc{i}", [128, DCH, H], BF16, c1) for i in range(2)]
                xa = [sbt(f"A_x{i}", [128, D], F32, c1) for i in range(2)]
                sq = [sbt(f"A_sq{i}", [128, D], F32, c1) for i in range(2)]
                xn_t = [sbt(f"A_xn{i}", [128, D], F32, c1) for i in range(2)]
                t1 = [sbt(f"A_t1{i}", [128, D], F32, c1) for i in range(2)]
                xna_t = [sbt(f"A_xna{i}", [128, D], BF16, c1) for i in range(2)]
                xnb_t = [sbt(f"A_xnb{i}", [128, D], BF16, c1) for i in range(2)]
                st5 = [[sbt(f"A_s{j}_{i}", [128, 1], F32, c1) for j in range(5)]
                       for i in range(2)]
                ph = Ph()
                lnw_tok = qop(ph, lambda q: q.dma_start(
                    out=w1bc[:], in_=ln1w.partition_broadcast(128)))
                lnw_tok = qop(ph, lambda q: q.dma_start(
                    out=b1bc[:], in_=ln1b.partition_broadcast(128)))
                pc = Precast(pcs, W1, w1_bf, "(c p) h -> p c h")
                cons_xa = [None, None]
                cons_xn = [[], []]
                cons_nm = [None, None]
                cons_xna = [None, None]
                cons_xn2b = [None, None]
                for t in range(NT):
                    p = t % 2
                    r0 = t * 128
                    ld = qop(ph, lambda q, p=p, r0=r0: q.dma_start(
                        out=xa[p][:], in_=x_in[r0:r0 + 128, :]), cons_xa[p])
                    vop(ph, lambda v, p=p: v.reduce_sum(
                        st5[p][0][:], xa[p][:], axis=AX), ld)
                    t_nm = vop(ph, lambda v, p=p: v.tensor_scalar_mul(
                        st5[p][1][:], st5[p][0][:], -1.0 / D), cons_nm[p])
                    t_sq = aop(ph, lambda a, p=p: a.activation(
                        sq[p][:], xa[p][:], AF.Square, bias=st5[p][1][:],
                        accum_out=st5[p][2][:]), t_nm, ld)
                    cons_xa[p] = t_sq
                    aop(ph, lambda a, p=p: a.activation(
                        st5[p][3][:], st5[p][2][:], AF.Ln, bias=eps_c[:],
                        scale=1.0 / D))
                    t_rs = aop(ph, lambda a, p=p: a.activation(
                        st5[p][4][:], st5[p][3][:], AF.Exp, scale=-0.5))
                    t_xn = vop(ph, lambda v, p=p: v.tensor_scalar(
                        xn_t[p][:], xa[p][:], st5[p][1][:], st5[p][4][:],
                        ALU.add, ALU.mult), t_rs, *cons_xn[p])
                    vop(ph, lambda v, p=p: v.reduce_sum(
                        st5[p][0][:], xn_t[p][:], axis=AX))
                    t_nm2 = vop(ph, lambda v, p=p: v.tensor_scalar_mul(
                        st5[p][1][:], st5[p][0][:], -1.0 / D), t_sq)
                    t_sq2 = aop(ph, lambda a, p=p: a.activation(
                        sq[p][:], xn_t[p][:], AF.Square, bias=st5[p][1][:],
                        accum_out=st5[p][2][:]), t_nm2, t_xn)
                    aop(ph, lambda a, p=p: a.activation(
                        st5[p][3][:], st5[p][2][:], AF.Ln, bias=eps_c[:],
                        scale=1.0 / D))
                    t_rs2 = aop(ph, lambda a, p=p: a.activation(
                        st5[p][4][:], st5[p][3][:], AF.Exp, scale=-0.5))
                    vop(ph, lambda v, p=p: v.tensor_scalar(
                        t1[p][:], xn_t[p][:], st5[p][1][:], st5[p][4][:],
                        ALU.add, ALU.mult), t_rs2)
                    vop(ph, lambda v, p=p: v.tensor_mul(t1[p][:], t1[p][:],
                                                        w1bc[:]), lnw_tok)
                    t_xna = vop(ph, lambda v, p=p: v.tensor_tensor(
                        xna_t[p][:], t1[p][:], b1bc[:], ALU.add), cons_xna[p])
                    t_xnb = vop(ph, lambda v, p=p: v.tensor_copy(
                        xnb_t[p][:], xn_t[p][:]), cons_xn2b[p])
                    st1 = qop(ph, lambda q, p=p, r0=r0: q.dma_start(
                        out=xn_f[r0:r0 + 128, :], in_=xn_t[p][:]), t_xn)
                    st2 = qop(ph, lambda q, p=p, r0=r0: q.dma_start(
                        out=xna_bf[r0:r0 + 128, :], in_=xna_t[p][:]), t_xna)
                    st3 = qop(ph, lambda q, p=p, r0=r0: q.dma_start(
                        out=xn_bf[r0:r0 + 128, :], in_=xnb_t[p][:]), t_xnb)
                    cons_xn[p] = [st1, t_sq2]
                    cons_xn2b[p] = st3
                    cons_nm[p] = t_sq2
                    cons_xna[p] = st2
                    if t % 2 == 1:
                        pc.step(ph)
                pc.finish(ph)
                a_store_q = q_s.cur()
                a_store_g = q_s.cur()
                run(ph, "phA")

        b_store = None
        if upto >= 2:
            with ExitStack() as c1:
                wt = {nm: sbt(f"B_W{nm}", [128, DCH, D], BF16, c1)
                      for nm in ("r", "k", "v")}
                xnaT = [sbt(f"B_xnaT{i}", [128, DCH, TSEQ], BF16, c1)
                        for i in range(2)]
                ev = [sbt(f"B_ev{i}", [128, 512], F32, c1) for i in range(4)]
                evb = [sbt(f"B_evb{i}", [128, 512], BF16, c1) for i in range(4)]
                pcs = [sbt(f"B_pc{i}", [128, DCH, H], BF16, c1) for i in range(2)]
                ph = Ph()
                wtok = None
                for nm, src in [("r", WrT), ("k", WkT), ("v", WvT)]:
                    wtok = gop(ph, lambda g, nm=nm, src=src: g.dma_start(
                        out=wt[nm][:],
                        in_=src.rearrange("(c p) o -> p c o", p=128)), a_store_g)
                pc = Precast(pcs, W3, w3_bf, "(c p) h -> p c h")
                bank_free = [None] * 4
                ev_free = [None] * 4
                blk_mm = [None, None]
                ld_tok = [None, None]
                for tb in range(NB):
                    pbk = tb % 2
                    t0 = tb * TSEQ
                    for dch in range(DCH):
                        ld_tok[pbk] = qop(ph, lambda q, pbk=pbk, dch=dch, t0=t0:
                                          q.dma_start_transpose(
                                              xnaT[pbk][:, dch, :],
                                              xna_bf[t0:t0 + TSEQ,
                                                     dch * 128:(dch + 1) * 128]),
                                          a_store_q, blk_mm[pbk])
                    gi = 0
                    for nm in ("k", "v", "r"):
                        dst = {"k": k_f, "v": v_f, "r": r_f}[nm]
                        for co in range(DCH):
                            for ts in range(4):
                                bank = gi % 4
                                ei = gi % 4
                                gi += 1

                                def mmg(t, nm=nm, co=co, ts=ts, bank=bank, pbk=pbk):
                                    inst = None
                                    for ci in range(DCH):
                                        inst = t.matmul(
                                            pb[bank][:],
                                            wt[nm][:, ci, co * 128:(co + 1) * 128],
                                            xnaT[pbk][:, ci,
                                                      ts * 512:(ts + 1) * 512],
                                            start=(ci == 0), stop=(ci == DCH - 1))
                                    return inst
                                mmtok = top(ph, mmg, ld_tok[pbk], wtok,
                                            bank_free[bank])
                                if nm == "r":
                                    etok = aop(ph, lambda a, bank=bank, ei=ei:
                                               a.activation(evb[ei][:], pb[bank][:],
                                                            AF.Sigmoid),
                                               mmtok, ev_free[ei])
                                    ebuf = evb[ei]
                                else:
                                    etok = vop(ph, lambda v, bank=bank, ei=ei:
                                               v.tensor_copy(ev[ei][:],
                                                             pb[bank][:]),
                                               mmtok, ev_free[ei])
                                    ebuf = ev[ei]
                                bank_free[bank] = etok
                                ev_free[ei] = qop(
                                    ph, lambda q, co=co, ts=ts, t0=t0,
                                    dst=dst, ebuf=ebuf:
                                    q.dma_start(
                                        out=dst[co, :,
                                                t0 + ts * 512:t0 + ts * 512 + 512],
                                        in_=ebuf[:]), etok)
                                if gi % 3 == 0:
                                    pc.step(ph)
                    blk_mm[pbk] = s_p.cur()
                pc.finish(ph)
                b_store = q_s.cur()
                run(ph, "phB")

        r_scat_done = None
        r_y_done = None
        if upto >= 3:
            with ExitStack() as c1:
                ws1 = sbt("R_Ws1", [128, DCH, D], BF16, c1)
                ws3 = sbt("R_Ws3", [128, DCH, D], BF16, c1)
                ws2 = sbt("R_Ws2", [128, DCH, D], BF16, c1)
                wg = sbt("R_Wg", [128, DCH, E], BF16, c1)
                xnT = [sbt("R_xnT0", [128, DCH, TSEQ], BF16, c1)] * 2
                hh = sbt("R_hh", [128, DCH, TSEQ], BF16, c1)
                silu_t = [sbt(f"R_silu{i}", [128, 512], F32, c1) for i in range(2)]
                zt = [sbt(f"R_z{i}", [128, D], F32, c1) for i in range(2)]
                mask_all = sbt("R_mask", [128, NT * E], F32, c1)
                pref_all = sbt("R_pref", [128, NT * E], F32, c1)
                sc_all = sbt("R_sc", [128, NT * E], F32, c1)
                mcum_all = sbt("R_mcum", [128, NT * E], F32, c1)
                toff_all = sbt("R_toff", [128, NT * E], F32, c1)
                cnt_a = sbt("R_cnta", [1, NT * E], F32, c1)
                cnt_b = sbt("R_cntb", [1, NT * E], F32, c1)
                rt = {}
                for nm, w_ in [("m", 1), ("nm", 1), ("sc", E), ("ss", 1),
                               ("rs", 1), ("m8", 8), ("gt", E), ("eq", E),
                               ("cg", 1), ("nd", 1), ("ca", E), ("cb", E),
                               ("sel", E)]:
                    rt[nm] = sbt(f"Rt_{nm}", [128, w_], F32, c1)
                slotm = sbt("R_slm", [128, E], F32, c1)
                t_oh = sbt("R_oh", [128, E], F32, c1)
                t_tmpE = sbt("R_tmpE", [128, E], F32, c1)
                slotf = [sbt(f"R_slf{i}", [128, 4], F32, c1) for i in range(2)]
                slotu = [sbt(f"R_slu{i}", [128, 4], U32, c1) for i in range(2)]
                packb = [sbt(f"R_pack{i}", [128, 8], F32, c1) for i in range(2)]
                idcol = sbt("R_idc", [128, 1], F32, c1)
                xn_sb = [sbt(f"R_xnsb{i}", [128, D], BF16, c1) for i in range(2)]
                pcs = [sbt(f"R_pc{i}", [128, 4, D], BF16, c1) for i in range(2)]

                ph = Ph()
                b_tail = [s_v.cur(), s_a.cur(), b_store]
                wtok = None
                for dst, src in [(ws1, Ws1), (ws3, Ws3), (ws2, Ws2), (wg, Wg)]:
                    wtok = gop(ph, lambda g, dst=dst, src=src: g.dma_start(
                        out=dst[:], in_=src.rearrange("(c p) o -> p c o", p=128)),
                        a_store_g, *b_tail)
                pc = Precast(pcs, W2, w2_bf, "(c p) d -> p c d")

                def cumsum_freeE(ph, src_ap, dst_ap):
                    cur = src_ap
                    lvls = [1, 2, 4, 8, 16]
                    for li, dd in enumerate(lvls):
                        nxt = (rt["ca"], rt["cb"])[li % 2][:] \
                            if li < len(lvls) - 1 else dst_ap
                        vop(ph, lambda v, nxt=nxt, cur=cur, dd=dd:
                            v.tensor_copy(nxt[:, 0:dd], cur[:, 0:dd]))
                        vop(ph, lambda v, nxt=nxt, cur=cur, dd=dd:
                            v.tensor_add(nxt[:, dd:E], cur[:, dd:E],
                                         cur[:, 0:E - dd]))
                        cur = nxt

                blk_use = [None, None]
                ld_tok = [None, None]
                gate_bank_free = [None, None]
                tri_bank_free = None
                for tb in range(NB):
                    pbk = tb % 2
                    t0 = tb * TSEQ
                    for dch in range(DCH):
                        ld_tok[pbk] = qop(ph, lambda q, pbk=pbk, dch=dch, t0=t0:
                                          q.dma_start_transpose(
                                              xnT[pbk][:, dch, :],
                                              xn_bf[t0:t0 + TSEQ,
                                                    dch * 128:(dch + 1) * 128]),
                                          a_store_g, a_store_q, blk_use[pbk],
                                          *b_tail)
                    for tt in range(16):
                        tile = tb * 16 + tt
                        cs = tile * E
                        gb = 6 + (tt % 2)

                        def gmm(t, tt=tt, pbk=pbk, gb=gb):
                            inst = None
                            for ci in range(DCH):
                                inst = t.matmul(
                                    pb[gb][:, 0:E],
                                    xnT[pbk][:, ci, tt * 128:(tt + 1) * 128],
                                    wg[:, ci, :],
                                    start=(ci == 0), stop=(ci == DCH - 1))
                            return inst
                        mmtok = top(ph, gmm, ld_tok[pbk], wtok,
                                    gate_bank_free[tt % 2])
                        vop(ph, lambda v, gb=gb: v.reduce_max(
                            rt["m"][:], pb[gb][:, 0:E], axis=AX), mmtok)
                        t_nm = vop(ph, lambda v: v.tensor_scalar_mul(
                            rt["nm"][:], rt["m"][:], -1.0))
                        t_exp = aop(ph, lambda a, gb=gb: a.activation(
                            rt["sc"][:], pb[gb][:, 0:E], AF.Exp,
                            bias=rt["nm"][:], accum_out=rt["ss"][:]), t_nm)
                        gate_bank_free[tt % 2] = t_exp
                        vop(ph, lambda v: v.reciprocal(rt["rs"][:], rt["ss"][:]),
                            t_exp)
                        vop(ph, lambda v, cs=cs: v.tensor_scalar_mul(
                            sc_all[:, cs:cs + E], rt["sc"][:], rt["rs"][:]))
                        vop(ph, lambda v, cs=cs: v.max(out=rt["m8"][:],
                                                       in_=sc_all[:, cs:cs + E]))
                        vop(ph, lambda v, cs=cs: v.tensor_single_scalar(
                            rt["gt"][:], sc_all[:, cs:cs + E], rt["m8"][:, 3:4],
                            ALU.is_gt))
                        vop(ph, lambda v, cs=cs: v.tensor_single_scalar(
                            rt["eq"][:], sc_all[:, cs:cs + E], rt["m8"][:, 3:4],
                            ALU.is_equal))
                        vop(ph, lambda v: v.reduce_sum(rt["cg"][:], rt["gt"][:],
                                                       axis=AX))
                        vop(ph, lambda v: v.tensor_scalar(
                            rt["nd"][:], rt["cg"][:], 4.0, -1.0,
                            ALU.subtract, ALU.mult))
                        cumsum_freeE(ph, rt["eq"][:], rt["sel"][:])
                        vop(ph, lambda v: v.tensor_single_scalar(
                            rt["sel"][:], rt["sel"][:], rt["nd"][:], ALU.is_le))
                        vop(ph, lambda v: v.tensor_mul(rt["sel"][:], rt["sel"][:],
                                                       rt["eq"][:]))
                        t_mask = vop(ph, lambda v, cs=cs: v.tensor_add(
                            mask_all[:, cs:cs + E], rt["gt"][:], rt["sel"][:]))
                        cumsum_freeE(ph, mask_all[:, cs:cs + E],
                                     mcum_all[:, cs:cs + E])

                        def pmm(t, cs=cs):
                            t.matmul(pb[5][:, 0:E], tri_f[:],
                                     mask_all[:, cs:cs + E], start=True, stop=True)
                            return t.matmul(pb[5][0:1, 32:32 + E], ones_c[:],
                                            mask_all[:, cs:cs + E],
                                            start=True, stop=True)
                        ptok = top(ph, pmm, t_mask, tri_bank_free)
                        vop(ph, lambda v, cs=cs: v.tensor_copy(
                            pref_all[:, cs:cs + E], pb[5][:, 0:E]), ptok)
                        tri_bank_free = vop(ph, lambda v, cs=cs: v.tensor_copy(
                            cnt_a[:, cs:cs + E], pb[5][0:1, 32:32 + E]))
                        if tt % 2 == 1:
                            pc.step(ph)

                    hpair_free = [None, None]
                    for hch in range(DCH):
                        for ts in range(4):
                            g2 = (hch * 4 + ts) % 2
                            b0, b1 = 2 * g2, 2 * g2 + 1

                            def hmm(t, hch=hch, ts=ts, pbk=pbk, b0=b0, b1=b1):
                                for ci in range(DCH):
                                    t.matmul(pb[b0][:],
                                             ws1[:, ci, hch * 128:(hch + 1) * 128],
                                             xnT[pbk][:, ci,
                                                      ts * 512:(ts + 1) * 512],
                                             start=(ci == 0), stop=(ci == DCH - 1))
                                inst = None
                                for ci in range(DCH):
                                    inst = t.matmul(
                                        pb[b1][:],
                                        ws3[:, ci, hch * 128:(hch + 1) * 128],
                                        xnT[pbk][:, ci, ts * 512:(ts + 1) * 512],
                                        start=(ci == 0), stop=(ci == DCH - 1))
                                return inst
                            mmtok = top(ph, hmm, ld_tok[pbk], wtok, hpair_free[g2])
                            stok = aop(ph, lambda a, b0=b0, g2=g2: a.activation(
                                silu_t[g2][:], pb[b0][:], AF.Silu), mmtok)
                            hpair_free[g2] = vop(
                                ph, lambda v, hch=hch, ts=ts, b1=b1, g2=g2:
                                v.tensor_tensor(hh[:, hch, ts * 512:(ts + 1) * 512],
                                                silu_t[g2][:], pb[b1][:], ALU.mult),
                                stok)
                    hh_done = s_v.cur()
                    zpair_free = [hpair_free[0], hpair_free[1]]
                    zt_free = [None, None]
                    for tt in range(16):
                        g2 = tt % 2
                        b0, b1 = 2 * g2, 2 * g2 + 1

                        def zmm(t, tt=tt, b0=b0, b1=b1):
                            inst = None
                            for bank, half in ((b0, 0), (b1, 1)):
                                for hc in range(DCH):
                                    inst = t.matmul(
                                        pb[bank][:],
                                        hh[:, hc, tt * 128:(tt + 1) * 128],
                                        ws2[:, hc, half * 512:(half + 1) * 512],
                                        start=(hc == 0), stop=(hc == DCH - 1))
                            return inst
                        mmtok = top(ph, zmm, hh_done, zpair_free[g2])
                        vop(ph, lambda v, b0=b0, g2=g2: v.tensor_copy(
                            zt[g2][:, 0:512], pb[b0][:]), mmtok, zt_free[g2])
                        zpair_free[g2] = vop(ph, lambda v, b1=b1, g2=g2:
                                             v.tensor_copy(zt[g2][:, 512:1024],
                                                           pb[b1][:]))
                        zt_free[g2] = qop(
                            ph, lambda q, t0=t0, tt=tt, g2=g2:
                            q.dma_start(
                                out=y_f[t0 + tt * 128:t0 + tt * 128 + 128, :],
                                in_=zt[g2][:]), zpair_free[g2])
                    blk_use[pbk] = s_p.cur()

                pc.finish(ph)
                pc_done = q_a.cur()
                bufs = [cnt_a.rearrange("o (t e) -> o t e", e=E),
                        cnt_b.rearrange("o (t e) -> o t e", e=E)]
                cur = bufs[0]
                for li, dd in enumerate([1, 2, 4, 8, 16, 32]):
                    nxt = bufs[(li + 1) % 2]
                    vop(ph, lambda v, nxt=nxt, cur=cur, dd=dd:
                        v.tensor_copy(nxt[:, 0:dd, :], cur[:, 0:dd, :]))
                    vop(ph, lambda v, nxt=nxt, cur=cur, dd=dd:
                        v.tensor_add(nxt[:, dd:NT, :], cur[:, dd:NT, :],
                                     cur[:, 0:NT - dd, :]))
                    cur = nxt
                vop(ph, lambda v: v.memset(bufs[1][:, 0:1, :], 0.0))
                t_ex = vop(ph, lambda v: v.tensor_copy(
                    bufs[1][:, 1:NT, :], bufs[0][:, 0:NT - 1, :]))
                bc_free = None
                for q4 in range(4):
                    btok = top(ph, lambda t, q4=q4: t.matmul(
                        pb[6][:, 0:512], ones_r[:],
                        cnt_b[:, q4 * 512:(q4 + 1) * 512], start=True, stop=True),
                        t_ex, bc_free, gate_bank_free[0])
                    bc_free = vop(ph, lambda v, q4=q4: v.tensor_copy(
                        toff_all[:, q4 * 512:(q4 + 1) * 512], pb[6][:, 0:512]),
                        btok)
                toff_done = bc_free

                scat_tok = [None, None]
                sstore_tok = [None, None]
                for tile in range(NT):
                    p2 = tile % 2
                    cs = tile * E
                    xldtok = qop(ph, lambda q, tile=tile, p2=p2: q.dma_start(
                        out=xn_sb[p2][:],
                        in_=xn_bf[tile * 128:(tile + 1) * 128, :]),
                        a_store_g, scat_tok[p2])
                    vop(ph, lambda v, cs=cs: v.tensor_add(
                        rt["ca"][:], iotaEM[:], toff_all[:, cs:cs + E]),
                        toff_done, scat_tok[p2])
                    vop(ph, lambda v, cs=cs: v.tensor_add(
                        rt["cb"][:], rt["ca"][:], pref_all[:, cs:cs + E]))
                    vop(ph, lambda v: v.tensor_tensor(slotm[:], rt["cb"][:],
                                                      clampM[:], ALU.min))
                    vop(ph, lambda v, tile=tile: v.tensor_scalar_add(
                        idcol[:], iota_p[:], float(tile * 128)))
                    for j in range(4):
                        vop(ph, lambda v, j=j, p2=p2: v.tensor_copy(
                            packb[p2][:, 2 * j:2 * j + 1], idcol[:]))
                        vop(ph, lambda v, j=j, cs=cs: v.tensor_single_scalar(
                            t_oh[:], mcum_all[:, cs:cs + E], float(j + 1),
                            ALU.is_equal))
                        vop(ph, lambda v, cs=cs: v.tensor_mul(
                            t_oh[:], t_oh[:], mask_all[:, cs:cs + E]))
                        vop(ph, lambda v: v.tensor_mul(t_tmpE[:], t_oh[:],
                                                       slotm[:]))
                        vop(ph, lambda v, j=j, p2=p2: v.reduce_sum(
                            slotf[p2][:, j:j + 1], t_tmpE[:], axis=AX))
                        vop(ph, lambda v, cs=cs: v.tensor_mul(
                            t_tmpE[:], t_oh[:], sc_all[:, cs:cs + E]))
                        vop(ph, lambda v, j=j, p2=p2: v.reduce_sum(
                            packb[p2][:, 2 * j + 1:2 * j + 2], t_tmpE[:], axis=AX))
                    t_slot = vop(ph, lambda v, p2=p2: v.tensor_copy(
                        slotu[p2][:], slotf[p2][:]), sstore_tok[p2])
                    sstore_tok[p2] = qop(ph, lambda q, tile=tile, p2=p2:
                                         q.dma_start(
                                             out=slots_tok[tile * 128:
                                                           (tile + 1) * 128, :],
                                             in_=slotu[p2][:]), t_slot)
                    for j in range(4):
                        gop(ph, lambda g, j=j, p2=p2: g.indirect_dma_start(
                            out=xg,
                            out_offset=IndirectOffsetOnAxis(
                                slotu[p2][:, j:j + 1], 0),
                            in_=xn_sb[p2][:], in_offset=None),
                            t_slot, xldtok)
                        gop(ph, lambda g, j=j, p2=p2: g.indirect_dma_start(
                            out=idw,
                            out_offset=IndirectOffsetOnAxis(
                                slotu[p2][:, j:j + 1], 0),
                            in_=packb[p2][:, 2 * j:2 * j + 2], in_offset=None))
                    scat_tok[p2] = q_g.cur()
                r_scat_done = q_g.cur()
                r_y_done = q_s.cur()
                r_slots_done = q_s.cur()
                run(ph, "phR")

        s_rw_done = None
        c_scat_done = None
        if upto >= 4:
            with ExitStack() as c1:
                vb1 = [sbt(f"S_v{i}", [128, TSEQ], F32, c1) for i in range(2)]
                zb1 = sbt("S_z", [128, TSEQ + 2], F32, c1)
                sb1 = [sbt(f"S_s{i}", [128, TSEQ + 2], F32, c1) for i in range(2)]
                srv1 = sbt("S_srv", [128, TSEQ + 1], F32, c1)
                zrv1 = sbt("S_zrv", [128, TSEQ + 1], F32, c1)
                Af1 = sbt("S_Af", [128, TSEQ], F32, c1)
                Bf1 = sbt("S_Bf", [128, TSEQ], F32, c1)
                Nf1 = sbt("S_Nf", [128, TSEQ], F32, c1)
                Df1 = sbt("S_Df", [128, TSEQ], F32, c1)
                rb1 = [sbt(f"S_r{i}", [128, TSEQ], BF16, c1) for i in range(2)]
                rwt1 = sbt("S_rw", [128, TSEQ], BF16, c1)
                xgT = [sbt(f"C_xgT{i}", [128, DCH, M], BF16, c1) for i in range(2)]
                h3t = [sbt(f"C_h3t{i}", [128, 512], BF16, c1) for i in range(2)]
                w1sb = [sbt(f"C_W1{i}", [128, DCH, H], BF16, c1) for i in range(2)]
                w3sb = [sbt(f"C_W3{i}", [128, DCH, H], BF16, c1) for i in range(2)]
                w2sb = [sbt("C_W2x", [128, 4, D], BF16, c1)] * 2
                hhE = sbt("C_hh", [128, 4, M], BF16, c1)
                silu_e = [sbt(f"C_silu{i}", [128, 512], BF16, c1) for i in range(2)]
                outsb = [sbt(f"C_out{i}", [128, D], BF16, c1) for i in range(4)]
                idwst = [sbt(f"C_idw{i}", [128, 10, 2], F32, c1) for i in range(2)]

                ph = Ph()
                r_tail = [s_v.cur(), s_a.cur(), r_slots_done, r_y_done]
                for buf in (zb1, sb1[0], sb1[1]):
                    vop(ph, lambda v, buf=buf: v.memset(buf[:, 0:1], 0.0))
                    vop(ph, lambda v, buf=buf: v.memset(
                        buf[:, TSEQ + 1:TSEQ + 2], 0.0))
                for buf in (srv1, zrv1):
                    vop(ph, lambda v, buf=buf: v.memset(buf[:, 0:1], 0.0))

                sb_free = [None, None]
                vb_free = [None, None]
                rb_free = [None, None]
                zb_free = [None]
                srw_st = None

                def sched_scan(ph, u):
                    nonlocal srw_st
                    p = u % 2
                    cch, sq = divmod(u, NSEQ)
                    c0 = sq * TSEQ
                    dbc = d_ex[:, cch:cch + 1].to_broadcast([128, TSEQ])
                    euc = eu[:, cch:cch + 1]
                    lk = qop(ph, lambda a, cch=cch, c0=c0, p=p: a.dma_start(
                        out=sb1[p][:, 1:TSEQ + 1],
                        in_=k_f[cch, :, c0:c0 + TSEQ]), b_store, sb_free[p])
                    lv = qop(ph, lambda a, cch=cch, c0=c0, p=p: a.dma_start(
                        out=vb1[p][:], in_=v_f[cch, :, c0:c0 + TSEQ]), vb_free[p])
                    lr = qop(ph, lambda a, cch=cch, c0=c0, p=p: a.dma_start(
                        out=rb1[p][:], in_=r_f[cch, :, c0:c0 + TSEQ]), rb_free[p])
                    ek = aop(ph, lambda a, p=p: a.activation(
                        zb1[:, 1:TSEQ + 1], sb1[p][:, 1:TSEQ + 1], AF.Exp),
                        lk, zb_free[0])
                    vop(ph, lambda v, p=p: v.tensor_tensor(
                        sb1[p][:, 1:TSEQ + 1], zb1[:, 1:TSEQ + 1], vb1[p][:],
                        ALU.mult), ek, lv)
                    vop(ph, lambda v, p=p: v.tensor_copy(
                        srv1[:, 1:TSEQ + 1], sb1[p][:, TSEQ:0:-1]))
                    vop(ph, lambda v: v.tensor_copy(
                        zrv1[:, 1:TSEQ + 1], zb1[:, TSEQ:0:-1]))
                    for dst, d1 in ((Af1, sb1[p]), (Bf1, zb1)):
                        vop(ph, lambda v, dst=dst, d1=d1, dbc=dbc:
                            v.tensor_tensor_scan(dst[:], dbc, d1[:, 0:TSEQ],
                                                 0.0, ALU.mult, ALU.add))
                        emit(ph, "vector", lambda v: v.engine_nop())
                    sb_free[p] = vop(ph, lambda v, euc=euc, p=p:
                                     v.scalar_tensor_tensor(
                                         Nf1[:], sb1[p][:, 1:TSEQ + 1], euc,
                                         Af1[:], ALU.mult, ALU.add))
                    zb_free[0] = vop(ph, lambda v, euc=euc:
                                     v.scalar_tensor_tensor(
                                         Df1[:], zb1[:, 1:TSEQ + 1], euc,
                                         Bf1[:], ALU.mult, ALU.add))
                    for dst, d1 in ((Af1, srv1), (Bf1, zrv1)):
                        vop(ph, lambda v, dst=dst, d1=d1, dbc=dbc:
                            v.tensor_tensor_scan(dst[:], dbc, d1[:, 0:TSEQ],
                                                 0.0, ALU.mult, ALU.add))
                        emit(ph, "vector", lambda v: v.engine_nop())
                    vop(ph, lambda v, euc=euc, p=p: v.scalar_tensor_tensor(
                        vb1[p][:], srv1[:, 1:TSEQ + 1], euc, Af1[:],
                        ALU.mult, ALU.add))
                    t_d2 = vop(ph, lambda v, euc=euc: v.scalar_tensor_tensor(
                        Af1[:], zrv1[:, 1:TSEQ + 1], euc, Bf1[:],
                        ALU.mult, ALU.add))
                    aop(ph, lambda a: a.activation(Bf1[:], Df1[:], AF.Ln), t_d2)
                    aop(ph, lambda a: a.activation(
                        Df1[:], Bf1[:], AF.Exp, scale=-1.0, bias=lnh_c[:]))
                    aop(ph, lambda a: a.activation(Bf1[:], Af1[:], AF.Ln))
                    t_rec = aop(ph, lambda a: a.activation(
                        Af1[:], Bf1[:], AF.Exp, scale=-1.0, bias=lnh_c[:]))
                    vop(ph, lambda v: v.tensor_mul(
                        srv1[:, 1:TSEQ + 1], Nf1[:], Df1[:]), t_rec, lr)
                    vb_free[p] = vop(ph, lambda v, p=p: v.tensor_mul(
                        Nf1[:], vb1[p][:], Af1[:]))
                    vop(ph, lambda v: v.tensor_add(
                        Df1[:], srv1[:, 1:TSEQ + 1], Nf1[:, ::-1]))
                    rb_free[p] = vop(ph, lambda v, p=p: v.tensor_tensor(
                        rwt1[:], Df1[:], rb1[p][:], ALU.mult), srw_st)
                    srw_st = sqop(ph, lambda a, cch=cch, c0=c0: a.dma_start(
                        out=rw_bf[cch, :, c0:c0 + TSEQ], in_=rwt1[:]),
                        rb_free[p])

                in_use = [None, None]
                hh_rd = None
                out_free = [None] * 4
                prev_wts_use = [None, None]
                hfree = [None, None]
                wfree = [None, None]

                def sched_expert(ph, e):
                    nonlocal hh_rd
                    p = e % 2
                    ldi = qop(ph, lambda q, e=e, p=p: q.dma_start(
                        out=idwst[p][:],
                        in_=idw[e * M:(e + 1) * M, :].rearrange(
                            "(a p) w -> p a w", p=128)),
                        r_scat_done, in_use[p], prev_wts_use[p], *r_tail)
                    lds = None
                    for dch in range(DCH):
                        lds = qop(ph, lambda q, e=e, p=p, dch=dch:
                                  q.dma_start_transpose(
                                      xgT[p][:, dch, :],
                                      xg[e * M:(e + 1) * M,
                                         dch * 128:(dch + 1) * 128]),
                                  r_scat_done, in_use[p])
                    ldw = None
                    for dst, srcw in [(w1sb, w1_bf), (w3sb, w3_bf)]:
                        ldw = qop(ph, lambda q, dst=dst, srcw=srcw, e=e, p=p:
                                  q.dma_start(out=dst[p][:], in_=srcw[e]),
                                  in_use[p], pc_done)
                    ldw2 = qop(ph, lambda q, e=e, p=p:
                               q.dma_start(out=w2sb[p][:], in_=w2_bf[e]),
                               hh_rd, pc_done)
                    for hs in range(4):
                        for rt_ in range(3):
                            r0 = rt_ * 512
                            rw_ = 512 if rt_ < 2 else M - 1024
                            g2 = (hs * 3 + rt_) % 2
                            b0, b1 = 2 * g2, 2 * g2 + 1

                            def wmm(t, hs=hs, r0=r0, rw_=rw_, p=p, b0=b0, b1=b1):
                                for ci in range(DCH):
                                    t.matmul(pb[b0][:, 0:rw_],
                                             w1sb[p][:, ci,
                                                     hs * 128:(hs + 1) * 128],
                                             xgT[p][:, ci, r0:r0 + rw_],
                                             start=(ci == 0),
                                             stop=(ci == DCH - 1))
                                inst = None
                                for ci in range(DCH):
                                    inst = t.matmul(
                                        pb[b1][:, 0:rw_],
                                        w3sb[p][:, ci, hs * 128:(hs + 1) * 128],
                                        xgT[p][:, ci, r0:r0 + rw_],
                                        start=(ci == 0), stop=(ci == DCH - 1))
                                return inst
                            mmtok = top(ph, wmm, lds, ldw, hfree[g2])
                            aop(ph, lambda a, b0=b0, rw_=rw_, g2=g2:
                                a.activation(silu_e[g2][:, 0:rw_],
                                             pb[b0][:, 0:rw_], AF.Silu),
                                mmtok, hfree[g2])
                            stok = aop(ph, lambda a, b1=b1, rw_=rw_, g2=g2:
                                       a.activation(h3t[g2][:, 0:rw_],
                                                    pb[b1][:, 0:rw_], AF.Copy),
                                       hfree[g2])
                            hfree[g2] = kop(
                                ph, lambda g, hs=hs, r0=r0, rw_=rw_, g2=g2:
                                g.tensor_tensor(hhE[:, hs, r0:r0 + rw_],
                                                silu_e[g2][:, 0:rw_],
                                                h3t[g2][:, 0:rw_], ALU.mult),
                                stok, hh_rd)
                    in_use[p] = s_p.cur()
                    hh_done = s_g.cur()
                    for rti in range(M // 128):
                        ob = rti % 4
                        g2 = rti % 2
                        b0, b1 = 4 + 2 * g2, 5 + 2 * g2

                        def w2mm(t, rti=rti, p=p, b0=b0, b1=b1):
                            inst = None
                            for bank, half in ((b0, 0), (b1, 1)):
                                for hs in range(4):
                                    inst = t.matmul(
                                        pb[bank][:],
                                        hhE[:, hs, rti * 128:(rti + 1) * 128],
                                        w2sb[p][:, hs,
                                                half * 512:(half + 1) * 512],
                                        start=(hs == 0), stop=(hs == 3))
                            return inst
                        mmtok = top(ph, w2mm, hh_done, wfree[g2], ldw2)
                        aop(ph, lambda a, rti=rti, ob=ob, b0=b0, p=p:
                            a.activation(outsb[ob][:, 0:512], pb[b0][:],
                                         AF.Copy, scale=idwst[p][:, rti, 1:2]),
                            mmtok, out_free[ob])
                        wfree[g2] = aop(
                            ph, lambda a, rti=rti, ob=ob, b1=b1, p=p:
                            a.activation(outsb[ob][:, 512:1024], pb[b1][:],
                                         AF.Copy, scale=idwst[p][:, rti, 1:2]))
                        evtok = s_a.cur()
                        out_free[ob] = sqop(
                            ph, lambda q, rti=rti, ob=ob, e=e:
                            q.dma_start(
                                out=out_slots[e * M + rti * 128:
                                              e * M + (rti + 1) * 128, :],
                                in_=outsb[ob][:]), evtok)
                    hh_rd = s_p.cur()
                    prev_wts_use[p] = s_a.cur()

                for i in range(E):
                    sched_expert(ph, i)
                    sched_scan(ph, i)
                s_rw_done = q_a.cur()
                c_scat_done = q_a.cur()
                run(ph, "phSC")

        if upto >= 6:
            with ExitStack() as c1:
                w2bc = sbt("w2bc", [128, D], F32, c1)
                b2bc = sbt("b2bc", [128, D], F32, c1)
                rwT = [sbt(f"D_rwT{i}", [128, DCH, TSEQ], BF16, c1)
                       for i in range(2)]
                woT = sbt("D_woT", [128, DCH, D], BF16, c1)
                xnt = [sbt(f"D_xn{i}", [128, D], F32, c1) for i in range(2)]
                xrt = [sbt(f"D_xr{i}", [128, D], F32, c1) for i in range(2)]
                yrt = [sbt(f"D_yr{i}", [128, D], F32, c1) for i in range(2)]
                a2t = [sbt(f"D_a2{i}", [128, D], F32, c1) for i in range(2)]
                axy = [sbt(f"D_axy{i}", [128, D], F32, c1) for i in range(2)]
                sqd = [sbt(f"D_sq{i}", [128, D], F32, c1) for i in range(2)]
                t1d = [sbt(f"D_t1{i}", [128, D], F32, c1) for i in range(2)]
                xn2t = [sbt(f"D_xn2{i}", [128, D], BF16, c1) for i in range(2)]
                st5 = [[sbt(f"D_s{j}_{i}", [128, 1], F32, c1) for j in range(5)]
                       for i in range(2)]
                ph = Ph()
                wtok = gop(ph, lambda g: g.dma_start(
                    out=woT[:], in_=WoT.rearrange("(c p) o -> p c o", p=128)))
                ln2_tok = qop(ph, lambda q: q.dma_start(
                    out=w2bc[:], in_=ln2w.partition_broadcast(128)))
                ln2_tok = qop(ph, lambda q: q.dma_start(
                    out=b2bc[:], in_=ln2b.partition_broadcast(128)))
                blk_use = [None, None]
                cons_xn = [None, None]
                cons_a2 = [None, None]
                cons_axy = [None, None]
                cons_x2 = [None, None]
                d1sq = [None, None]
                sc_tail = [s_a.cur(), s_g.cur(), s_v.cur()]
                for tb in range(NB):
                    pbk = tb % 2
                    t0 = tb * TSEQ
                    ldr = None
                    for cch in range(DCH):
                        ldr = qop(ph, lambda q, pbk=pbk, cch=cch, t0=t0:
                                  q.dma_start(
                                      out=rwT[pbk][:, cch, :],
                                      in_=rw_bf[cch, :, t0:t0 + TSEQ]),
                                  s_rw_done, blk_use[pbk], *sc_tail,
                                  c_scat_done)
                    for tt in range(16):
                        p = tt % 2
                        r0 = t0 + tt * 128
                        b0, b1 = 2 * p, 2 * p + 1
                        ldx = qop(ph, lambda q, p=p, r0=r0: q.dma_start(
                            out=xnt[p][:], in_=xn_f[r0:r0 + 128, :]), cons_xn[p])
                        qop(ph, lambda q, p=p, r0=r0: q.dma_start(
                            out=xrt[p][:], in_=x_in[r0:r0 + 128, :]),
                            cons_axy[p])
                        ldy = qop(ph, lambda q, p=p, r0=r0: q.dma_start(
                            out=yrt[p][:], in_=y_f[r0:r0 + 128, :]), r_y_done)

                        def omm(t, tt=tt, pbk=pbk, b0=b0, b1=b1):
                            inst = None
                            for bank, half in ((b0, 0), (b1, 1)):
                                for cch in range(DCH):
                                    inst = t.matmul(
                                        pb[bank][:],
                                        rwT[pbk][:, cch, tt * 128:(tt + 1) * 128],
                                        woT[:, cch, half * 512:(half + 1) * 512],
                                        start=(cch == 0), stop=(cch == DCH - 1))
                            return inst
                        mmtok = top(ph, omm, ldr, wtok, cons_a2[p],
                                    *sc_tail, s_v.cur(), s_a.cur(),
                                    q_s.cur(), q_a.cur())
                        vop(ph, lambda v, p=p, b0=b0: v.tensor_add(
                            a2t[p][:, 0:512], xnt[p][:, 0:512], pb[b0][:]),
                            mmtok, ldx)
                        t_a2 = vop(ph, lambda v, p=p, b1=b1: v.tensor_add(
                            a2t[p][:, 512:1024], xnt[p][:, 512:1024], pb[b1][:]))
                        vop(ph, lambda v, p=p: v.reduce_sum(
                            st5[p][0][:], a2t[p][:], axis=AX))
                        t_nm = vop(ph, lambda v, p=p: v.tensor_scalar_mul(
                            st5[p][1][:], st5[p][0][:], -1.0 / D), d1sq[p])
                        t_sq = aop(ph, lambda a, p=p: a.activation(
                            sqd[p][:], a2t[p][:], AF.Square, bias=st5[p][1][:],
                            accum_out=st5[p][2][:]), t_nm)
                        d1sq[p] = t_sq
                        aop(ph, lambda a, p=p: a.activation(
                            st5[p][3][:], st5[p][2][:], AF.Ln, bias=eps_c[:],
                            scale=1.0 / D))
                        t_rs = aop(ph, lambda a, p=p: a.activation(
                            st5[p][4][:], st5[p][3][:], AF.Exp, scale=-0.5))
                        vop(ph, lambda v, p=p: v.tensor_scalar(
                            t1d[p][:], a2t[p][:], st5[p][1][:], st5[p][4][:],
                            ALU.add, ALU.mult), t_rs)
                        vop(ph, lambda v, p=p: v.tensor_mul(
                            t1d[p][:], t1d[p][:], w2bc[:]), ln2_tok)
                        t_x2 = vop(ph, lambda v, p=p: v.tensor_tensor(
                            xn2t[p][:], t1d[p][:], b2bc[:], ALU.add),
                            cons_x2[p])
                        vop(ph, lambda v, p=p: v.tensor_add(
                            axy[p][:], a2t[p][:], xrt[p][:]), ldy, cons_a2[p])
                        t_axy = vop(ph, lambda v, p=p: v.tensor_add(
                            axy[p][:], axy[p][:], yrt[p][:]))
                        cons_axy[p] = t_axy
                        st1 = qop(ph, lambda q, p=p, r0=r0: q.dma_start(
                            out=a2xy_f[r0:r0 + 128, :], in_=axy[p][:]), t_axy)
                        st2 = qop(ph, lambda q, p=p, r0=r0: q.dma_start(
                            out=xn2_bf[r0:r0 + 128, :], in_=xn2t[p][:]), t_x2)
                        cons_a2[p] = st1
                        cons_xn[p] = st2
                        cons_x2[p] = st2
                    blk_use[pbk] = s_p.cur()
                d1_store = q_s.cur()
                run(ph, "phD1")

            with ExitStack() as c1:
                wfk = sbt("D_wfk", [128, DCH, HK], BF16, c1)
                xn2T = [sbt(f"D_x2T{i}", [128, DCH, TSEQ], BF16, c1)
                        for i in range(2)]
                relu_t = [sbt(f"D_rel{i}", [128, 512], F32, c1) for i in range(3)]
                kst = [sbt(f"D_kst{i}", [128, 512], BF16, c1) for i in range(3)]
                ph = Ph()
                wtok = gop(ph, lambda g: g.dma_start(
                    out=wfk[:], in_=WfkT.rearrange("(c p) o -> p c o", p=128)))
                blk_use = [None, None]
                bank_free = [None] * 4
                buf_free = [None] * 3
                for tb in range(NB):
                    pbk = tb % 2
                    t0 = tb * TSEQ
                    ldt = None
                    for dch in range(DCH):
                        ldt = qop(ph, lambda q, pbk=pbk, dch=dch, t0=t0:
                                  q.dma_start_transpose(
                                      xn2T[pbk][:, dch, :],
                                      xn2_bf[t0:t0 + TSEQ,
                                             dch * 128:(dch + 1) * 128]),
                                  d1_store, blk_use[pbk])
                    gi = 0
                    for hkc in range(HKCH):
                        for ts in range(4):
                            bank = gi % 4
                            bi = gi % 3
                            gi += 1

                            def kmm(t, hkc=hkc, ts=ts, pbk=pbk, bank=bank):
                                inst = None
                                for ci in range(DCH):
                                    inst = t.matmul(
                                        pb[bank][:],
                                        wfk[:, ci, hkc * 128:(hkc + 1) * 128],
                                        xn2T[pbk][:, ci, ts * 512:(ts + 1) * 512],
                                        start=(ci == 0), stop=(ci == DCH - 1))
                                return inst
                            mmtok = top(ph, kmm, ldt, wtok, bank_free[bank])
                            bank_free[bank] = vop(
                                ph, lambda v, bank=bank, bi=bi:
                                v.tensor_scalar_max(relu_t[bi][:], pb[bank][:],
                                                    0.0),
                                mmtok, buf_free[bi])
                            t_sqr = aop(ph, lambda a, bi=bi: a.activation(
                                kst[bi][:], relu_t[bi][:], AF.Square),
                                bank_free[bank])
                            buf_free[bi] = qop(
                                ph, lambda q, hkc=hkc, ts=ts, t0=t0, bi=bi:
                                q.dma_start(
                                    out=kk_bf[hkc, :,
                                              t0 + ts * 512:t0 + ts * 512 + 512],
                                    in_=kst[bi][:]), t_sqr)
                    blk_use[pbk] = s_p.cur()
                d2a_store = q_s.cur()
                run(ph, "phD2a")

            with ExitStack() as c1:
                wfv = sbt("D_wfv", [128, HKCH, D], BF16, c1)
                wfr = sbt("D_wfr", [128, DCH, D], BF16, c1)
                kks = [sbt(f"D_kks{i}", [128, HKCH, 512], BF16, c1)
                       for i in range(2)]
                xn2Th = [sbt(f"D_x2h{i}", [128, DCH, 512], BF16, c1)
                         for i in range(2)]
                a2l = [sbt(f"D_a2l{i}", [128, D], F32, c1) for i in range(2)]
                sg = [sbt(f"D_sg{i}", [128, D], F32, c1) for i in range(2)]
                ot = [sbt(f"D_ot{i}", [128, D], F32, c1) for i in range(2)]
                gbuf = [sbt(f"D_gb{i}", [128, D], BF16, c1) for i in range(8)]
                stl = [sbt(f"D_stl{i}", [128, 4], U32, c1) for i in range(2)]
                ph = Ph()
                wtok = gop(ph, lambda g: g.dma_start(
                    out=wfv[:], in_=WfvT.rearrange("(c p) o -> p c o", p=128)),
                    c_scat_done)
                wtok = gop(ph, lambda g: g.dma_start(
                    out=wfr[:], in_=WfrT.rearrange("(c p) o -> p c o", p=128)))
                blk_use = [None, None]
                cons = [None, None]
                out_st = [None, None]
                gts_tok = [None, None]
                for hb in range(16):
                    pbk = hb % 2
                    h0 = hb * 512
                    ldk = None
                    for hkc in range(HKCH):
                        ldk = qop(ph, lambda q, pbk=pbk, hkc=hkc, h0=h0:
                                  q.dma_start(
                                      out=kks[pbk][:, hkc, :],
                                      in_=kk_bf[hkc, :, h0:h0 + 512]),
                                  d2a_store, blk_use[pbk])
                    ldt = None
                    for dch in range(DCH):
                        ldt = qop(ph, lambda q, pbk=pbk, dch=dch, h0=h0:
                                  q.dma_start_transpose(
                                      xn2Th[pbk][:, dch, :],
                                      xn2_bf[h0:h0 + 512,
                                             dch * 128:(dch + 1) * 128]),
                                  d1_store, blk_use[pbk])
                    for tt in range(4):
                        p = tt % 2
                        r0 = h0 + tt * 128
                        fb0, fb1 = 2 * p, 2 * p + 1
                        rb0, rb1 = 4 + 2 * p, 5 + 2 * p
                        ldx = qop(ph, lambda q, p=p, r0=r0: q.dma_start(
                            out=a2l[p][:], in_=a2xy_f[r0:r0 + 128, :]), cons[p])
                        lds = qop(ph, lambda q, p=p, r0=r0: q.dma_start(
                            out=stl[p][:], in_=slots_tok[r0:r0 + 128, :]),
                            cons[p], r_slots_done, gts_tok[p])
                        gts = []
                        for j in range(4):
                            gts.append(gop(
                                ph, lambda g, p=p, j=j:
                                g.indirect_dma_start(
                                    out=gbuf[4 * p + j][:], out_offset=None,
                                    in_=out_slots,
                                    in_offset=IndirectOffsetOnAxis(
                                        stl[p][:, j:j + 1], 0)),
                                lds, c_scat_done, cons[p]))
                        gts_tok[p] = q_g.cur()

                        def fmm(t, tt=tt, pbk=pbk, fb0=fb0, fb1=fb1,
                                rb0=rb0, rb1=rb1):
                            for bank, half in ((fb0, 0), (fb1, 1)):
                                for hkc in range(HKCH):
                                    t.matmul(
                                        pb[bank][:],
                                        kks[pbk][:, hkc,
                                                 tt * 128:(tt + 1) * 128],
                                        wfv[:, hkc, half * 512:(half + 1) * 512],
                                        start=(hkc == 0),
                                        stop=(hkc == HKCH - 1))
                            inst = None
                            for bank, half in ((rb0, 0), (rb1, 1)):
                                for ci in range(DCH):
                                    inst = t.matmul(
                                        pb[bank][:],
                                        xn2Th[pbk][:, ci,
                                                   tt * 128:(tt + 1) * 128],
                                        wfr[:, ci, half * 512:(half + 1) * 512],
                                        start=(ci == 0), stop=(ci == DCH - 1))
                            return inst
                        mmtok = top(ph, fmm, ldk, ldt, wtok, cons[p])
                        aop(ph, lambda a, p=p, rb0=rb0: a.activation(
                            sg[p][:, 0:512], pb[rb0][:], AF.Sigmoid), mmtok)
                        t_sg = aop(ph, lambda a, p=p, rb1=rb1: a.activation(
                            sg[p][:, 512:1024], pb[rb1][:], AF.Sigmoid))
                        vop(ph, lambda v, p=p, fb0=fb0: v.tensor_tensor(
                            ot[p][:, 0:512], sg[p][:, 0:512], pb[fb0][:],
                            ALU.mult), t_sg, mmtok, out_st[p])
                        vop(ph, lambda v, p=p, fb1=fb1: v.tensor_tensor(
                            ot[p][:, 512:1024], sg[p][:, 512:1024], pb[fb1][:],
                            ALU.mult))
                        vop(ph, lambda v, p=p: v.tensor_add(
                            ot[p][:], ot[p][:], a2l[p][:]), ldx)
                        for j in range(4):
                            vop(ph, lambda v, p=p, j=j: v.tensor_add(
                                ot[p][:], ot[p][:], gbuf[4 * p + j][:]),
                                gts[j])
                        t_o = s_v.cur()
                        cons[p] = s_v.cur()
                        out_st[p] = qop(ph, lambda q, p=p, r0=r0: q.dma_start(
                            out=out_d[r0:r0 + 128, :], in_=ot[p][:]), t_o)
                    blk_use[pbk] = s_p.cur()
                run(ph, "phD2b")

        ph = Ph()
        emit(ph, "sync", lambda q: q.nop(), waits=allcur())
        run(ph, "fin")

    return nc


_cache = {}


def _get_prog(upto=99, debug=False):
    key = (upto, debug)
    if key not in _cache:
        _cache[key] = build(upto=upto, debug=debug)
    return _cache[key]


def _prep_inputs(inputs):
    f = np.ascontiguousarray

    def g(name):
        return np.asarray(inputs[name])

    x = g("L0_attn_x").reshape(32, 2048, 1024)
    base = {
        "ln1w": f(g("L0_attn_ln1_w").reshape(1, D)),
        "ln1b": f(g("L0_attn_ln1_b").reshape(1, D)),
        "ln2w": f(g("L0_attn_ln2_w").reshape(1, D)),
        "ln2b": f(g("L0_attn_ln2_b").reshape(1, D)),
        "WrT": f(g("L0_attn_Wr").T),
        "WkT": f(g("L0_attn_Wk").T),
        "WvT": f(g("L0_attn_Wv").T),
        "WoT": f(g("L0_attn_Wo").T),
        "WfkT": f(g("L0_attn_Wfk").T),
        "WfvT": f(g("L0_attn_Wfv").T),
        "WfrT": f(g("L0_attn_Wfr").T),
        "Wg": f(g("L0_ffn_Wg")),
        "Ws1": f(g("L0_ffn_Ws1")),
        "Ws3": f(g("L0_ffn_Ws3")),
        "Ws2": f(g("L0_ffn_Ws2")),
        "W1": f(g("L0_ffn_W1")),
        "W3": f(g("L0_ffn_W3")),
        "W2": f(g("L0_ffn_W2")),
        "decay": f(g("L0_attn_decay").reshape(DCH, 128).T),
        "u": f(g("L0_attn_u").reshape(DCH, 128).T),
        "tri": np.triu(np.ones((128, 128), np.float32), 1),
        "onesc": np.ones((128, 1), np.float32),
        "onesr": np.ones((1, 128), np.float32),
        "iotaEM_i": np.broadcast_to((np.arange(E) * M).astype(np.float32),
                                    (128, E)).copy(),
        "iotap_i": np.arange(128, dtype=np.float32).reshape(128, 1),
    }
    maps = []
    for c in range(NCORES):
        m = dict(base)
        m["x"] = f(x[c * NSEQ:(c + 1) * NSEQ].reshape(TC, D))
        maps.append(m)
    return maps


def kernel(**inputs):
    nc = _get_prog()
    in_maps = _prep_inputs(inputs)
    res = run_bass_kernel_spmd(nc, in_maps, core_ids=list(range(NCORES)))
    out = np.stack([np.asarray(r["out"]).reshape(NSEQ, 2048, D)
                    for r in res.results])
    return out.reshape(32, 2048, D)


# version 29
# speedup vs baseline: 1.1828x; 1.1828x over previous
import numpy as np
from contextlib import ExitStack

import concourse.bass as bass
import concourse.mybir as mybir
from concourse.bass import IndirectOffsetOnAxis
from concourse.bass_utils import run_bass_kernel_spmd

F32 = mybir.dt.float32
BF16 = mybir.dt.bfloat16
U32 = mybir.dt.uint32
AF = mybir.ActivationFunctionType
ALU = mybir.AluOpType
AX = mybir.AxisListType.X

NCORES = 8
TSEQ = 2048
NSEQ = 4
TC = NSEQ * TSEQ
D = 1024
DCH = 8
E = 32
H = 512
HK = 4096
HKCH = 32
M = 1280
NT = TC // 128
NB = 4
EPS = 1e-5
LN_HALF = float(np.log(0.5))

ENGINES = ("sync", "vector", "scalar", "tensor", "gpsimd")


class Sem:
    __slots__ = ("h", "n")

    def __init__(self, h):
        self.h = h
        self.n = 0

    def cur(self):
        return (self, self.n)


def _merge_waits(waits):
    best = {}
    for w in waits:
        if w is None:
            continue
        s, v = w
        if v <= 0:
            continue
        if best.get(s.h, -1) < v:
            best[s.h] = v
    return list(best.items())


class Ph:
    def __init__(self):
        self.q = {e: [] for e in ENGINES}

    def flush(self, block):
        for e in ENGINES:
            fns = self.q[e]
            if fns:
                def runner(X, fns=fns):
                    for f in fns:
                        f(X)
                getattr(block, e)(runner)


def build(upto=99, debug=False):
    nc = bass.Bass("TRN2", target_bir_lowering=False)

    def din(name, shape, dtype=F32):
        return nc.dram_tensor(name, shape, dtype, kind="ExternalInput").ap()

    x_in = din("x", [TC, D])
    ln1w = din("ln1w", [1, D]);  ln1b = din("ln1b", [1, D])
    ln2w = din("ln2w", [1, D]);  ln2b = din("ln2b", [1, D])
    WrT = din("WrT", [D, D]);  WkT = din("WkT", [D, D])
    WvT = din("WvT", [D, D]);  WoT = din("WoT", [D, D])
    WfkT = din("WfkT", [D, HK]); WfvT = din("WfvT", [HK, D]); WfrT = din("WfrT", [D, D])
    Wg = din("Wg", [D, E])
    Ws1 = din("Ws1", [D, D]); Ws3 = din("Ws3", [D, D]); Ws2 = din("Ws2", [D, D])
    W1 = din("W1", [E, D, H]); W3 = din("W3", [E, D, H]); W2 = din("W2", [E, H, D])
    decay = din("decay", [128, DCH]); u_in = din("u", [128, DCH])
    tri_in = din("tri", [128, 128])
    onesc = din("onesc", [128, 1]); onesr = din("onesr", [1, 128])
    iotaEM_in = din("iotaEM_i", [128, E]); iotap_in = din("iotap_i", [128, 1])

    sk = "ExternalOutput" if debug else "Internal"

    def dsc(name, shape, dtype=F32):
        return nc.dram_tensor(name, shape, dtype, kind=sk).ap()

    xn_f = dsc("xn_f", [TC, D])
    xn_bf = dsc("xn_bf", [TC, D], BF16)
    xna_bf = dsc("xna_bf", [TC, D], BF16)
    k_f = dsc("k_f", [DCH, 128, TC])
    v_f = dsc("v_f", [DCH, 128, TC])
    r_f = dsc("r_f", [DCH, 128, TC], BF16)
    rw_bf = dsc("rw_bf", [DCH, 128, TC], BF16)
    xg = dsc("xg_s", [E * M, D], BF16)
    idw = dsc("idw_s", [E * M, 2])
    y_f = dsc("y_f", [TC + 128, D])
    a2_f = dsc("a2_f", [TC, D])
    xn2_bf = dsc("xn2_bf", [TC, D], BF16)
    kk_bf = dsc("kk_bf", [HKCH, 128, TC], BF16)
    out_slots = dsc("oslot", [E * M, D], BF16)
    slots_tok = dsc("stok", [TC, 4], U32)
    a2xy_f = dsc("a2xy_f", [TC, D])
    w1_bf = dsc("w1_bf", [E, 128, DCH, H], BF16)
    w3_bf = dsc("w3_bf", [E, 128, DCH, H], BF16)
    w2_bf = dsc("w2_bf", [E, 128, 4, D], BF16)
    out_d = nc.dram_tensor("out", [TC, D], F32, kind="ExternalOutput").ap()

    with ExitStack() as ctx0:
        def sbt(name, shape, dtype=F32, stack=None):
            return (stack or ctx0).enter_context(nc.sbuf_tensor(name, shape, dtype))

        d_ex = sbt("d_ex", [128, DCH]); eu = sbt("eu", [128, DCH])
        tmp8 = sbt("tmp8", [128, DCH])
        tri_f = sbt("tri_f", [128, 128])
        ones_c = sbt("ones_c", [128, 1]); ones_r = sbt("ones_r", [1, 128])
        iotaEM = sbt("iotaEM", [128, E]); clampM = sbt("clampM", [128, E])
        iota_p = sbt("iota_p", [128, 1])
        eps_c = sbt("eps_c", [128, 1]); lnh_c = sbt("lnh_c", [128, 1])

        pb = [ctx0.enter_context(nc.psum_tensor(f"pb{i}", [128, 512], F32))
              for i in range(8)]

        q_s = Sem(ctx0.enter_context(nc.semaphore("q_s")))
        q_g = Sem(ctx0.enter_context(nc.semaphore("q_g")))
        s_v = Sem(ctx0.enter_context(nc.semaphore("s_v")))
        s_a = Sem(ctx0.enter_context(nc.semaphore("s_a")))
        s_p = Sem(ctx0.enter_context(nc.semaphore("s_p")))
        q_a = Sem(ctx0.enter_context(nc.semaphore("q_a")))
        s_g = Sem(ctx0.enter_context(nc.semaphore("s_g")))
        allsems = [q_s, q_g, s_v, s_a, s_p, q_a, s_g]

        def dsem(name, n=1):
            out = []
            for i in range(n):
                s = Sem(ctx0.enter_context(nc.semaphore(f"{name}{i}")))
                allsems.append(s)
                out.append(s)
            return out if n > 1 else out[0]

        def emit(ph, eng, build_fn, waits=(), inc=None, inck=1):
            wl = _merge_waits(waits)
            if inc is not None:
                inc.n += inck
                ih = inc.h
            else:
                ih = None

            def fn(X, wl=wl, build_fn=build_fn, ih=ih, inck=inck):
                for h, v in wl:
                    X.wait_ge(h, v)
                inst = build_fn(X)
                if ih is not None:
                    inst.then_inc(ih, inck)
            ph.q[eng].append(fn)

        def vop(ph, bf_, *extra):
            emit(ph, "vector", bf_, waits=[s_v.cur(), *extra], inc=s_v)
            return s_v.cur()

        def aop(ph, bf_, *extra):
            emit(ph, "scalar", bf_, waits=[s_a.cur(), *extra], inc=s_a)
            return s_a.cur()

        def top(ph, bf_, *extra):
            emit(ph, "tensor", bf_, waits=list(extra), inc=s_p)
            return s_p.cur()

        def qop(ph, bf_, *extra, sem=None):
            s = sem or q_s
            emit(ph, "sync", bf_, waits=list(extra), inc=s, inck=16)
            return s.cur()

        def gop(ph, bf_, *extra, sem=None):
            s = sem or q_g
            emit(ph, "gpsimd", bf_, waits=list(extra), inc=s, inck=16)
            return s.cur()

        def sqop(ph, bf_, *extra, sem=None):
            s = sem or q_a
            emit(ph, "scalar", bf_, waits=list(extra), inc=s, inck=16)
            return s.cur()

        def kop(ph, bf_, *extra):
            emit(ph, "gpsimd", bf_, waits=[s_g.cur(), *extra], inc=s_g)
            return s_g.cur()

        def allcur():
            return [s.cur() for s in allsems]

        class Precast:
            def __init__(self, stage, src_w, dst_w, rrs, lsems, ssems):
                self.stage = stage
                self.src_w = src_w
                self.dst_w = dst_w
                self.rrs = rrs
                self.i = 0
                self.stok = [None, None]
                self.lsems = lsems
                self.ssems = ssems

            def step(self, ph):
                if self.i >= E:
                    return
                e = self.i
                pp = e % 2
                lt = gop(ph, lambda g, e=e, pp=pp, s=self:
                         g.dma_start(out=s.stage[pp][:],
                                     in_=s.src_w[e].rearrange(s.rrs, p=128)),
                         self.stok[pp], sem=self.lsems[pp])
                self.stok[pp] = qop(ph, lambda q, e=e, pp=pp, s=self:
                                    q.dma_start(out=s.dst_w[e],
                                                in_=s.stage[pp][:]), lt,
                                    sem=self.ssems[pp])
                self.i += 1

            def finish(self, ph):
                while self.i < E:
                    self.step(ph)

        def run(ph, name="ph"):
            with nc.named_scope(name), nc.Block() as block:
                ph.flush(block)
                waits = _merge_waits(allcur())
                for e in ENGINES:
                    def bar(X, waits=waits):
                        for h, v in waits:
                            X.wait_ge(h, v)
                    getattr(block, e)(bar)

        prep_stack = ExitStack()
        sent = sbt("sent", [128, (E * M * 2) // 128], F32, prep_stack)
        ph = Ph()
        for dst, src in [(d_ex, decay), (eu, u_in), (tri_f, tri_in),
                         (ones_c, onesc), (ones_r, onesr),
                         (iotaEM, iotaEM_in), (iota_p, iotap_in)]:
            qop(ph, lambda q, dst=dst, src=src: q.dma_start(out=dst[:], in_=src))
        ldp = q_s.cur()
        aop(ph, lambda a: a.activation(tmp8[:], d_ex[:], AF.Exp), ldp)
        aop(ph, lambda a: a.activation(d_ex[:], tmp8[:], AF.Exp, scale=-1.0))
        t_eu = aop(ph, lambda a: a.activation(tmp8[:], eu[:], AF.Exp))
        vop(ph, lambda v: v.tensor_copy(eu[:], tmp8[:]), t_eu)
        vop(ph, lambda v: v.tensor_scalar_add(clampM[:], iotaEM[:], float(M - 1)), ldp)
        vop(ph, lambda v: v.memset(eps_c[:], EPS))
        vop(ph, lambda v: v.memset(lnh_c[:], LN_HALF))
        t_sent = vop(ph, lambda v: v.memset(sent[:], float(TC)))
        qop(ph, lambda q: q.dma_start(
            out=idw.rearrange("(p a) w -> p (a w)", p=128), in_=sent[:]), t_sent)
        run(ph, "prep")
        prep_stack.close()

        a_store_q = None
        a_store_g = None
        if upto >= 1:
            with ExitStack() as c1:
                w1bc = sbt("w1bc", [128, D], F32, c1)
                b1bc = sbt("b1bc", [128, D], F32, c1)
                pcs = [sbt(f"A_pc{i}", [128, DCH, H], BF16, c1) for i in range(2)]
                xa = [sbt(f"A_x{i}", [128, D], F32, c1) for i in range(2)]
                sq = [sbt(f"A_sq{i}", [128, D], F32, c1) for i in range(2)]
                xn_t = [sbt(f"A_xn{i}", [128, D], F32, c1) for i in range(2)]
                t1 = [sbt(f"A_t1{i}", [128, D], F32, c1) for i in range(2)]
                xna_t = [sbt(f"A_xna{i}", [128, D], BF16, c1) for i in range(2)]
                xnb_t = [sbt(f"A_xnb{i}", [128, D], BF16, c1) for i in range(2)]
                st5 = [[sbt(f"A_s{j}_{i}", [128, 1], F32, c1) for j in range(5)]
                       for i in range(2)]
                ph = Ph()
                lnw_tok = qop(ph, lambda q: q.dma_start(
                    out=w1bc[:], in_=ln1w.partition_broadcast(128)))
                lnw_tok = qop(ph, lambda q: q.dma_start(
                    out=b1bc[:], in_=ln1b.partition_broadcast(128)))
                sA_x = dsem("sA_x", 2)
                sA_st = dsem("sA_st", 2)
                sA_st2 = dsem("sA_st2", 2)
                sA_pcl = dsem("sA_pcl", 2)
                sA_pcs = dsem("sA_pcs", 2)
                pc = Precast(pcs, W1, w1_bf, "(c p) h -> p c h", sA_pcl, sA_pcs)
                cons_xa = [None, None]
                cons_xn = [[], []]
                cons_nm = [None, None]
                cons_xna = [None, None]
                cons_xn2b = [None, None]
                for t in range(NT):
                    p = t % 2
                    r0 = t * 128
                    ld = qop(ph, lambda q, p=p, r0=r0: q.dma_start(
                        out=xa[p][:], in_=x_in[r0:r0 + 128, :]), cons_xa[p],
                        sem=sA_x[p])
                    vop(ph, lambda v, p=p: v.reduce_sum(
                        st5[p][0][:], xa[p][:], axis=AX), ld)
                    t_nm = vop(ph, lambda v, p=p: v.tensor_scalar_mul(
                        st5[p][1][:], st5[p][0][:], -1.0 / D), cons_nm[p])
                    t_sq = aop(ph, lambda a, p=p: a.activation(
                        sq[p][:], xa[p][:], AF.Square, bias=st5[p][1][:],
                        accum_out=st5[p][2][:]), t_nm, ld)
                    cons_xa[p] = t_sq
                    aop(ph, lambda a, p=p: a.activation(
                        st5[p][3][:], st5[p][2][:], AF.Ln, bias=eps_c[:],
                        scale=1.0 / D))
                    t_rs = aop(ph, lambda a, p=p: a.activation(
                        st5[p][4][:], st5[p][3][:], AF.Exp, scale=-0.5))
                    t_xn = vop(ph, lambda v, p=p: v.tensor_scalar(
                        xn_t[p][:], xa[p][:], st5[p][1][:], st5[p][4][:],
                        ALU.add, ALU.mult), t_rs, *cons_xn[p])
                    vop(ph, lambda v, p=p: v.reduce_sum(
                        st5[p][0][:], xn_t[p][:], axis=AX))
                    t_nm2 = vop(ph, lambda v, p=p: v.tensor_scalar_mul(
                        st5[p][1][:], st5[p][0][:], -1.0 / D), t_sq)
                    t_sq2 = aop(ph, lambda a, p=p: a.activation(
                        sq[p][:], xn_t[p][:], AF.Square, bias=st5[p][1][:],
                        accum_out=st5[p][2][:]), t_nm2, t_xn)
                    aop(ph, lambda a, p=p: a.activation(
                        st5[p][3][:], st5[p][2][:], AF.Ln, bias=eps_c[:],
                        scale=1.0 / D))
                    t_rs2 = aop(ph, lambda a, p=p: a.activation(
                        st5[p][4][:], st5[p][3][:], AF.Exp, scale=-0.5))
                    vop(ph, lambda v, p=p: v.tensor_scalar(
                        t1[p][:], xn_t[p][:], st5[p][1][:], st5[p][4][:],
                        ALU.add, ALU.mult), t_rs2)
                    vop(ph, lambda v, p=p: v.tensor_mul(t1[p][:], t1[p][:],
                                                        w1bc[:]), lnw_tok)
                    t_xna = vop(ph, lambda v, p=p: v.tensor_tensor(
                        xna_t[p][:], t1[p][:], b1bc[:], ALU.add), cons_xna[p])
                    t_xnb = vop(ph, lambda v, p=p: v.tensor_copy(
                        xnb_t[p][:], xn_t[p][:]), cons_xn2b[p])
                    st1 = qop(ph, lambda q, p=p, r0=r0: q.dma_start(
                        out=xn_f[r0:r0 + 128, :], in_=xn_t[p][:]), t_xn)
                    st2 = qop(ph, lambda q, p=p, r0=r0: q.dma_start(
                        out=xna_bf[r0:r0 + 128, :], in_=xna_t[p][:]), t_xna)
                    st3 = qop(ph, lambda q, p=p, r0=r0: q.dma_start(
                        out=xn_bf[r0:r0 + 128, :], in_=xnb_t[p][:]), t_xnb)
                    cons_xn[p] = [st1, t_sq2]
                    cons_xn2b[p] = st3
                    cons_nm[p] = t_sq2
                    cons_xna[p] = st2
                    if t % 2 == 1:
                        pc.step(ph)
                pc.finish(ph)
                a_store_q = None
                a_store_g = None
                run(ph, "phA")

        b_store = None
        if upto >= 2:
            with ExitStack() as c1:
                wt = {nm: sbt(f"B_W{nm}", [128, DCH, D], BF16, c1)
                      for nm in ("r", "k", "v")}
                xnaT = [sbt(f"B_xnaT{i}", [128, DCH, TSEQ], BF16, c1)
                        for i in range(2)]
                ev = [sbt(f"B_ev{i}", [128, 512], F32, c1) for i in range(4)]
                evb = [sbt(f"B_evb{i}", [128, 512], BF16, c1) for i in range(4)]
                pcs = [sbt(f"B_pc{i}", [128, DCH, H], BF16, c1) for i in range(2)]
                ph = Ph()
                sB_xn = dsem("sB_xn", 2)
                sB_ev = dsem("sB_ev", 4)
                sB_w = dsem("sB_w")
                sB_pcl = dsem("sB_pcl", 2)
                sB_pcs = dsem("sB_pcs", 2)
                wtok = None
                for nm, src in [("r", WrT), ("k", WkT), ("v", WvT)]:
                    wtok = gop(ph, lambda g, nm=nm, src=src: g.dma_start(
                        out=wt[nm][:],
                        in_=src.rearrange("(c p) o -> p c o", p=128)),
                        a_store_g, sem=sB_w)
                pc = Precast(pcs, W3, w3_bf, "(c p) h -> p c h", sB_pcl, sB_pcs)
                bank_free = [None] * 4
                ev_free = [None] * 4
                blk_mm = [None, None]
                ld_tok = [None, None]
                for tb in range(NB):
                    pbk = tb % 2
                    t0 = tb * TSEQ
                    for dch in range(DCH):
                        ld_tok[pbk] = qop(ph, lambda q, pbk=pbk, dch=dch, t0=t0:
                                          q.dma_start_transpose(
                                              xnaT[pbk][:, dch, :],
                                              xna_bf[t0:t0 + TSEQ,
                                                     dch * 128:(dch + 1) * 128]),
                                          a_store_q, blk_mm[pbk],
                                          sem=sB_xn[pbk])
                    gi = 0
                    for nm in ("k", "v", "r"):
                        dst = {"k": k_f, "v": v_f, "r": r_f}[nm]
                        for co in range(DCH):
                            for ts in range(4):
                                bank = gi % 4
                                ei = gi % 4
                                gi += 1

                                def mmg(t, nm=nm, co=co, ts=ts, bank=bank, pbk=pbk):
                                    inst = None
                                    for ci in range(DCH):
                                        inst = t.matmul(
                                            pb[bank][:],
                                            wt[nm][:, ci, co * 128:(co + 1) * 128],
                                            xnaT[pbk][:, ci,
                                                      ts * 512:(ts + 1) * 512],
                                            start=(ci == 0), stop=(ci == DCH - 1))
                                    return inst
                                mmtok = top(ph, mmg, ld_tok[pbk], wtok,
                                            bank_free[bank])
                                if nm == "r":
                                    etok = aop(ph, lambda a, bank=bank, ei=ei:
                                               a.activation(evb[ei][:], pb[bank][:],
                                                            AF.Sigmoid),
                                               mmtok, ev_free[ei])
                                    ebuf = evb[ei]
                                else:
                                    etok = vop(ph, lambda v, bank=bank, ei=ei:
                                               v.tensor_copy(ev[ei][:],
                                                             pb[bank][:]),
                                               mmtok, ev_free[ei])
                                    ebuf = ev[ei]
                                bank_free[bank] = etok
                                ev_free[ei] = qop(
                                    ph, lambda q, co=co, ts=ts, t0=t0,
                                    dst=dst, ebuf=ebuf:
                                    q.dma_start(
                                        out=dst[co, :,
                                                t0 + ts * 512:t0 + ts * 512 + 512],
                                        in_=ebuf[:]), etok, sem=sB_ev[ei])
                                if gi % 3 == 0:
                                    pc.step(ph)
                    blk_mm[pbk] = s_p.cur()
                pc.finish(ph)
                b_store = None
                run(ph, "phB")

        r_scat_done = None
        r_y_done = None
        if upto >= 3:
            with ExitStack() as c1:
                ws1 = sbt("R_Ws1", [128, DCH, D], BF16, c1)
                ws3 = sbt("R_Ws3", [128, DCH, D], BF16, c1)
                ws2 = sbt("R_Ws2", [128, DCH, D], BF16, c1)
                wg = sbt("R_Wg", [128, DCH, E], BF16, c1)
                xnT = [sbt("R_xnT0", [128, DCH, TSEQ], BF16, c1)] * 2
                hh = sbt("R_hh", [128, DCH, TSEQ], BF16, c1)
                silu_t = [sbt(f"R_silu{i}", [128, 512], F32, c1) for i in range(2)]
                zt = [sbt(f"R_z{i}", [128, D], F32, c1) for i in range(2)]
                mask_all = sbt("R_mask", [128, NT * E], F32, c1)
                pref_all = sbt("R_pref", [128, NT * E], F32, c1)
                sc_all = sbt("R_sc", [128, NT * E], F32, c1)
                mcum_all = sbt("R_mcum", [128, NT * E], F32, c1)
                toff_all = sbt("R_toff", [128, NT * E], F32, c1)
                cnt_a = sbt("R_cnta", [1, NT * E], F32, c1)
                cnt_b = sbt("R_cntb", [1, NT * E], F32, c1)
                rt = {}
                for nm, w_ in [("m", 1), ("nm", 1), ("sc", E), ("ss", 1),
                               ("rs", 1), ("m8", 8), ("gt", E), ("eq", E),
                               ("cg", 1), ("nd", 1), ("ca", E), ("cb", E),
                               ("sel", E)]:
                    rt[nm] = sbt(f"Rt_{nm}", [128, w_], F32, c1)
                slotm = sbt("R_slm", [128, E], F32, c1)
                t_oh = sbt("R_oh", [128, E], F32, c1)
                t_tmpE = sbt("R_tmpE", [128, E], F32, c1)
                slotf = [sbt(f"R_slf{i}", [128, 4], F32, c1) for i in range(2)]
                slotu = [sbt(f"R_slu{i}", [128, 4], U32, c1) for i in range(2)]
                packb = [sbt(f"R_pack{i}", [128, 8], F32, c1) for i in range(2)]
                idcol = sbt("R_idc", [128, 1], F32, c1)
                xn_sb = [sbt(f"R_xnsb{i}", [128, D], BF16, c1) for i in range(2)]
                pcs = [sbt(f"R_pc{i}", [128, 4, D], BF16, c1) for i in range(2)]

                ph = Ph()
                sR_xn = dsem("sR_xn")
                sR_w = dsem("sR_w")
                sR_z = dsem("sR_z", 2)
                sR_xs = dsem("sR_xs", 2)
                sR_ss = dsem("sR_ss", 2)
                sR_sc = dsem("sR_sc", 2)
                sR_pcl = dsem("sR_pcl", 2)
                sR_pcs = dsem("sR_pcs", 2)
                b_tail = [s_v.cur(), s_a.cur(), b_store]
                wtok = None
                for dst, src in [(ws1, Ws1), (ws3, Ws3), (ws2, Ws2), (wg, Wg)]:
                    wtok = gop(ph, lambda g, dst=dst, src=src: g.dma_start(
                        out=dst[:], in_=src.rearrange("(c p) o -> p c o", p=128)),
                        a_store_g, *b_tail, sem=sR_w)
                pc = Precast(pcs, W2, w2_bf, "(c p) d -> p c d", sR_pcl, sR_pcs)

                def cumsum_freeE(ph, src_ap, dst_ap):
                    cur = src_ap
                    lvls = [1, 2, 4, 8, 16]
                    for li, dd in enumerate(lvls):
                        nxt = (rt["ca"], rt["cb"])[li % 2][:] \
                            if li < len(lvls) - 1 else dst_ap
                        vop(ph, lambda v, nxt=nxt, cur=cur, dd=dd:
                            v.tensor_copy(nxt[:, 0:dd], cur[:, 0:dd]))
                        vop(ph, lambda v, nxt=nxt, cur=cur, dd=dd:
                            v.tensor_add(nxt[:, dd:E], cur[:, dd:E],
                                         cur[:, 0:E - dd]))
                        cur = nxt

                blk_use = [None, None]
                ld_tok = [None, None]
                gate_bank_free = [None, None]
                tri_bank_free = None
                for tb in range(NB):
                    pbk = tb % 2
                    t0 = tb * TSEQ
                    for dch in range(DCH):
                        ld_tok[pbk] = qop(ph, lambda q, pbk=pbk, dch=dch, t0=t0:
                                          q.dma_start_transpose(
                                              xnT[pbk][:, dch, :],
                                              xn_bf[t0:t0 + TSEQ,
                                                    dch * 128:(dch + 1) * 128]),
                                          a_store_g, a_store_q, blk_use[pbk],
                                          *b_tail, sem=sR_xn)
                    for tt in range(16):
                        tile = tb * 16 + tt
                        cs = tile * E
                        gb = 6 + (tt % 2)

                        def gmm(t, tt=tt, pbk=pbk, gb=gb):
                            inst = None
                            for ci in range(DCH):
                                inst = t.matmul(
                                    pb[gb][:, 0:E],
                                    xnT[pbk][:, ci, tt * 128:(tt + 1) * 128],
                                    wg[:, ci, :],
                                    start=(ci == 0), stop=(ci == DCH - 1))
                            return inst
                        mmtok = top(ph, gmm, ld_tok[pbk], wtok,
                                    gate_bank_free[tt % 2])
                        vop(ph, lambda v, gb=gb: v.reduce_max(
                            rt["m"][:], pb[gb][:, 0:E], axis=AX), mmtok)
                        t_nm = vop(ph, lambda v: v.tensor_scalar_mul(
                            rt["nm"][:], rt["m"][:], -1.0))
                        t_exp = aop(ph, lambda a, gb=gb: a.activation(
                            rt["sc"][:], pb[gb][:, 0:E], AF.Exp,
                            bias=rt["nm"][:], accum_out=rt["ss"][:]), t_nm)
                        gate_bank_free[tt % 2] = t_exp
                        vop(ph, lambda v: v.reciprocal(rt["rs"][:], rt["ss"][:]),
                            t_exp)
                        vop(ph, lambda v, cs=cs: v.tensor_scalar_mul(
                            sc_all[:, cs:cs + E], rt["sc"][:], rt["rs"][:]))
                        vop(ph, lambda v, cs=cs: v.max(out=rt["m8"][:],
                                                       in_=sc_all[:, cs:cs + E]))
                        vop(ph, lambda v, cs=cs: v.tensor_single_scalar(
                            rt["gt"][:], sc_all[:, cs:cs + E], rt["m8"][:, 3:4],
                            ALU.is_gt))
                        vop(ph, lambda v, cs=cs: v.tensor_single_scalar(
                            rt["eq"][:], sc_all[:, cs:cs + E], rt["m8"][:, 3:4],
                            ALU.is_equal))
                        vop(ph, lambda v: v.reduce_sum(rt["cg"][:], rt["gt"][:],
                                                       axis=AX))
                        vop(ph, lambda v: v.tensor_scalar(
                            rt["nd"][:], rt["cg"][:], 4.0, -1.0,
                            ALU.subtract, ALU.mult))
                        cumsum_freeE(ph, rt["eq"][:], rt["sel"][:])
                        vop(ph, lambda v: v.tensor_single_scalar(
                            rt["sel"][:], rt["sel"][:], rt["nd"][:], ALU.is_le))
                        vop(ph, lambda v: v.tensor_mul(rt["sel"][:], rt["sel"][:],
                                                       rt["eq"][:]))
                        t_mask = vop(ph, lambda v, cs=cs: v.tensor_add(
                            mask_all[:, cs:cs + E], rt["gt"][:], rt["sel"][:]))
                        cumsum_freeE(ph, mask_all[:, cs:cs + E],
                                     mcum_all[:, cs:cs + E])

                        def pmm(t, cs=cs):
                            t.matmul(pb[5][:, 0:E], tri_f[:],
                                     mask_all[:, cs:cs + E], start=True, stop=True)
                            return t.matmul(pb[5][0:1, 32:32 + E], ones_c[:],
                                            mask_all[:, cs:cs + E],
                                            start=True, stop=True)
                        ptok = top(ph, pmm, t_mask, tri_bank_free)
                        vop(ph, lambda v, cs=cs: v.tensor_copy(
                            pref_all[:, cs:cs + E], pb[5][:, 0:E]), ptok)
                        tri_bank_free = vop(ph, lambda v, cs=cs: v.tensor_copy(
                            cnt_a[:, cs:cs + E], pb[5][0:1, 32:32 + E]))
                        if tt % 2 == 1:
                            pc.step(ph)

                    hpair_free = [None, None]
                    for hch in range(DCH):
                        for ts in range(4):
                            g2 = (hch * 4 + ts) % 2
                            b0, b1 = 2 * g2, 2 * g2 + 1

                            def hmm(t, hch=hch, ts=ts, pbk=pbk, b0=b0, b1=b1):
                                for ci in range(DCH):
                                    t.matmul(pb[b0][:],
                                             ws1[:, ci, hch * 128:(hch + 1) * 128],
                                             xnT[pbk][:, ci,
                                                      ts * 512:(ts + 1) * 512],
                                             start=(ci == 0), stop=(ci == DCH - 1))
                                inst = None
                                for ci in range(DCH):
                                    inst = t.matmul(
                                        pb[b1][:],
                                        ws3[:, ci, hch * 128:(hch + 1) * 128],
                                        xnT[pbk][:, ci, ts * 512:(ts + 1) * 512],
                                        start=(ci == 0), stop=(ci == DCH - 1))
                                return inst
                            mmtok = top(ph, hmm, ld_tok[pbk], wtok, hpair_free[g2])
                            stok = aop(ph, lambda a, b0=b0, g2=g2: a.activation(
                                silu_t[g2][:], pb[b0][:], AF.Silu), mmtok)
                            hpair_free[g2] = vop(
                                ph, lambda v, hch=hch, ts=ts, b1=b1, g2=g2:
                                v.tensor_tensor(hh[:, hch, ts * 512:(ts + 1) * 512],
                                                silu_t[g2][:], pb[b1][:], ALU.mult),
                                stok)
                    hh_done = s_v.cur()
                    zpair_free = [hpair_free[0], hpair_free[1]]
                    zt_free = [None, None]
                    for tt in range(16):
                        g2 = tt % 2
                        b0, b1 = 2 * g2, 2 * g2 + 1

                        def zmm(t, tt=tt, b0=b0, b1=b1):
                            inst = None
                            for bank, half in ((b0, 0), (b1, 1)):
                                for hc in range(DCH):
                                    inst = t.matmul(
                                        pb[bank][:],
                                        hh[:, hc, tt * 128:(tt + 1) * 128],
                                        ws2[:, hc, half * 512:(half + 1) * 512],
                                        start=(hc == 0), stop=(hc == DCH - 1))
                            return inst
                        mmtok = top(ph, zmm, hh_done, zpair_free[g2])
                        vop(ph, lambda v, b0=b0, g2=g2: v.tensor_copy(
                            zt[g2][:, 0:512], pb[b0][:]), mmtok, zt_free[g2])
                        zpair_free[g2] = vop(ph, lambda v, b1=b1, g2=g2:
                                             v.tensor_copy(zt[g2][:, 512:1024],
                                                           pb[b1][:]))
                        zt_free[g2] = qop(
                            ph, lambda q, t0=t0, tt=tt, g2=g2:
                            q.dma_start(
                                out=y_f[t0 + tt * 128:t0 + tt * 128 + 128, :],
                                in_=zt[g2][:]), zpair_free[g2], sem=sR_z[g2])
                    blk_use[pbk] = s_p.cur()

                pc.finish(ph)
                pc_done = None
                bufs = [cnt_a.rearrange("o (t e) -> o t e", e=E),
                        cnt_b.rearrange("o (t e) -> o t e", e=E)]
                cur = bufs[0]
                for li, dd in enumerate([1, 2, 4, 8, 16, 32]):
                    nxt = bufs[(li + 1) % 2]
                    vop(ph, lambda v, nxt=nxt, cur=cur, dd=dd:
                        v.tensor_copy(nxt[:, 0:dd, :], cur[:, 0:dd, :]))
                    vop(ph, lambda v, nxt=nxt, cur=cur, dd=dd:
                        v.tensor_add(nxt[:, dd:NT, :], cur[:, dd:NT, :],
                                     cur[:, 0:NT - dd, :]))
                    cur = nxt
                vop(ph, lambda v: v.memset(bufs[1][:, 0:1, :], 0.0))
                t_ex = vop(ph, lambda v: v.tensor_copy(
                    bufs[1][:, 1:NT, :], bufs[0][:, 0:NT - 1, :]))
                bc_free = None
                for q4 in range(4):
                    btok = top(ph, lambda t, q4=q4: t.matmul(
                        pb[6][:, 0:512], ones_r[:],
                        cnt_b[:, q4 * 512:(q4 + 1) * 512], start=True, stop=True),
                        t_ex, bc_free, gate_bank_free[0])
                    bc_free = vop(ph, lambda v, q4=q4: v.tensor_copy(
                        toff_all[:, q4 * 512:(q4 + 1) * 512], pb[6][:, 0:512]),
                        btok)
                toff_done = bc_free

                scat_tok = [None, None]
                sstore_tok = [None, None]
                for tile in range(NT):
                    p2 = tile % 2
                    cs = tile * E
                    xldtok = qop(ph, lambda q, tile=tile, p2=p2: q.dma_start(
                        out=xn_sb[p2][:],
                        in_=xn_bf[tile * 128:(tile + 1) * 128, :]),
                        a_store_g, scat_tok[p2], sem=sR_xs[p2])
                    vop(ph, lambda v, cs=cs: v.tensor_add(
                        rt["ca"][:], iotaEM[:], toff_all[:, cs:cs + E]),
                        toff_done, scat_tok[p2])
                    vop(ph, lambda v, cs=cs: v.tensor_add(
                        rt["cb"][:], rt["ca"][:], pref_all[:, cs:cs + E]))
                    vop(ph, lambda v: v.tensor_tensor(slotm[:], rt["cb"][:],
                                                      clampM[:], ALU.min))
                    vop(ph, lambda v, tile=tile: v.tensor_scalar_add(
                        idcol[:], iota_p[:], float(tile * 128)))
                    for j in range(4):
                        vop(ph, lambda v, j=j, p2=p2: v.tensor_copy(
                            packb[p2][:, 2 * j:2 * j + 1], idcol[:]))
                        vop(ph, lambda v, j=j, cs=cs: v.tensor_single_scalar(
                            t_oh[:], mcum_all[:, cs:cs + E], float(j + 1),
                            ALU.is_equal))
                        vop(ph, lambda v, cs=cs: v.tensor_mul(
                            t_oh[:], t_oh[:], mask_all[:, cs:cs + E]))
                        vop(ph, lambda v: v.tensor_mul(t_tmpE[:], t_oh[:],
                                                       slotm[:]))
                        vop(ph, lambda v, j=j, p2=p2: v.reduce_sum(
                            slotf[p2][:, j:j + 1], t_tmpE[:], axis=AX))
                        vop(ph, lambda v, cs=cs: v.tensor_mul(
                            t_tmpE[:], t_oh[:], sc_all[:, cs:cs + E]))
                        vop(ph, lambda v, j=j, p2=p2: v.reduce_sum(
                            packb[p2][:, 2 * j + 1:2 * j + 2], t_tmpE[:], axis=AX))
                    t_slot = vop(ph, lambda v, p2=p2: v.tensor_copy(
                        slotu[p2][:], slotf[p2][:]), sstore_tok[p2])
                    sstore_tok[p2] = qop(ph, lambda q, tile=tile, p2=p2:
                                         q.dma_start(
                                             out=slots_tok[tile * 128:
                                                           (tile + 1) * 128, :],
                                             in_=slotu[p2][:]), t_slot,
                                         sem=sR_ss[p2])
                    for j in range(4):
                        gop(ph, lambda g, j=j, p2=p2: g.indirect_dma_start(
                            out=xg,
                            out_offset=IndirectOffsetOnAxis(
                                slotu[p2][:, j:j + 1], 0),
                            in_=xn_sb[p2][:], in_offset=None),
                            t_slot, xldtok, sem=sR_sc[p2])
                        gop(ph, lambda g, j=j, p2=p2: g.indirect_dma_start(
                            out=idw,
                            out_offset=IndirectOffsetOnAxis(
                                slotu[p2][:, j:j + 1], 0),
                            in_=packb[p2][:, 2 * j:2 * j + 2], in_offset=None),
                            sem=sR_sc[p2])
                    scat_tok[p2] = sR_sc[p2].cur()
                r_scat_done = None
                r_y_done = None
                r_slots_done = None
                run(ph, "phR")

        s_rw_done = None
        c_scat_done = None
        if upto >= 4:
            with ExitStack() as c1:
                vb1 = [sbt(f"S_v{i}", [128, TSEQ], F32, c1) for i in range(2)]
                zb1 = sbt("S_z", [128, TSEQ + 2], F32, c1)
                sb1 = [sbt(f"S_s{i}", [128, TSEQ + 2], F32, c1) for i in range(2)]
                srv1 = sbt("S_srv", [128, TSEQ + 1], F32, c1)
                zrv1 = sbt("S_zrv", [128, TSEQ + 1], F32, c1)
                Af1 = sbt("S_Af", [128, TSEQ], F32, c1)
                Bf1 = sbt("S_Bf", [128, TSEQ], F32, c1)
                Nf1 = sbt("S_Nf", [128, TSEQ], F32, c1)
                Df1 = sbt("S_Df", [128, TSEQ], F32, c1)
                rb1 = [sbt(f"S_r{i}", [128, TSEQ], BF16, c1) for i in range(2)]
                rwt1 = sbt("S_rw", [128, TSEQ], BF16, c1)
                xgT = [sbt(f"C_xgT{i}", [128, DCH, M], BF16, c1) for i in range(2)]
                h3t = [sbt(f"C_h3t{i}", [128, 512], BF16, c1) for i in range(2)]
                w1sb = [sbt(f"C_W1{i}", [128, DCH, H], BF16, c1) for i in range(2)]
                w3sb = [sbt(f"C_W3{i}", [128, DCH, H], BF16, c1) for i in range(2)]
                w2sb = [sbt("C_W2x", [128, 4, D], BF16, c1)] * 2
                hhE = sbt("C_hh", [128, 4, M], BF16, c1)
                silu_e = [sbt(f"C_silu{i}", [128, 512], BF16, c1) for i in range(2)]
                outsb = [sbt(f"C_out{i}", [128, D], BF16, c1) for i in range(4)]
                idwst = [sbt(f"C_idw{i}", [128, 10, 2], F32, c1) for i in range(2)]

                ph = Ph()
                sS_k = dsem("sS_k", 2)
                sS_v = dsem("sS_v", 2)
                sS_r = dsem("sS_r", 2)
                sS_rw = dsem("sS_rw")
                sC_x = dsem("sC_x", 2)
                sC_w = dsem("sC_w", 2)
                sC_w2 = dsem("sC_w2")
                sC_i = dsem("sC_i", 2)
                sC_o = dsem("sC_o", 4)
                r_tail = [s_v.cur(), s_a.cur()]
                for buf in (zb1, sb1[0], sb1[1]):
                    vop(ph, lambda v, buf=buf: v.memset(buf[:, 0:1], 0.0))
                    vop(ph, lambda v, buf=buf: v.memset(
                        buf[:, TSEQ + 1:TSEQ + 2], 0.0))
                for buf in (srv1, zrv1):
                    vop(ph, lambda v, buf=buf: v.memset(buf[:, 0:1], 0.0))

                sb_free = [None, None]
                vb_free = [None, None]
                rb_free = [None, None]
                zb_free = [None]
                srw_st = None

                def sched_scan(ph, u):
                    nonlocal srw_st
                    p = u % 2
                    cch, sq = divmod(u, NSEQ)
                    c0 = sq * TSEQ
                    dbc = d_ex[:, cch:cch + 1].to_broadcast([128, TSEQ])
                    euc = eu[:, cch:cch + 1]
                    lk = qop(ph, lambda a, cch=cch, c0=c0, p=p: a.dma_start(
                        out=sb1[p][:, 1:TSEQ + 1],
                        in_=k_f[cch, :, c0:c0 + TSEQ]), b_store, sb_free[p])
                    lv = qop(ph, lambda a, cch=cch, c0=c0, p=p: a.dma_start(
                        out=vb1[p][:], in_=v_f[cch, :, c0:c0 + TSEQ]), vb_free[p])
                    lr = qop(ph, lambda a, cch=cch, c0=c0, p=p: a.dma_start(
                        out=rb1[p][:], in_=r_f[cch, :, c0:c0 + TSEQ]), rb_free[p])
                    ek = aop(ph, lambda a, p=p: a.activation(
                        zb1[:, 1:TSEQ + 1], sb1[p][:, 1:TSEQ + 1], AF.Exp),
                        lk, zb_free[0])
                    vop(ph, lambda v, p=p: v.tensor_tensor(
                        sb1[p][:, 1:TSEQ + 1], zb1[:, 1:TSEQ + 1], vb1[p][:],
                        ALU.mult), ek, lv)
                    vop(ph, lambda v, p=p: v.tensor_copy(
                        srv1[:, 1:TSEQ + 1], sb1[p][:, TSEQ:0:-1]))
                    vop(ph, lambda v: v.tensor_copy(
                        zrv1[:, 1:TSEQ + 1], zb1[:, TSEQ:0:-1]))
                    for dst, d1 in ((Af1, sb1[p]), (Bf1, zb1)):
                        vop(ph, lambda v, dst=dst, d1=d1, dbc=dbc:
                            v.tensor_tensor_scan(dst[:], dbc, d1[:, 0:TSEQ],
                                                 0.0, ALU.mult, ALU.add))
                        emit(ph, "vector", lambda v: v.engine_nop())
                    sb_free[p] = vop(ph, lambda v, euc=euc, p=p:
                                     v.scalar_tensor_tensor(
                                         Nf1[:], sb1[p][:, 1:TSEQ + 1], euc,
                                         Af1[:], ALU.mult, ALU.add))
                    zb_free[0] = vop(ph, lambda v, euc=euc:
                                     v.scalar_tensor_tensor(
                                         Df1[:], zb1[:, 1:TSEQ + 1], euc,
                                         Bf1[:], ALU.mult, ALU.add))
                    for dst, d1 in ((Af1, srv1), (Bf1, zrv1)):
                        vop(ph, lambda v, dst=dst, d1=d1, dbc=dbc:
                            v.tensor_tensor_scan(dst[:], dbc, d1[:, 0:TSEQ],
                                                 0.0, ALU.mult, ALU.add))
                        emit(ph, "vector", lambda v: v.engine_nop())
                    vop(ph, lambda v, euc=euc, p=p: v.scalar_tensor_tensor(
                        vb1[p][:], srv1[:, 1:TSEQ + 1], euc, Af1[:],
                        ALU.mult, ALU.add))
                    t_d2 = vop(ph, lambda v, euc=euc: v.scalar_tensor_tensor(
                        Af1[:], zrv1[:, 1:TSEQ + 1], euc, Bf1[:],
                        ALU.mult, ALU.add))
                    aop(ph, lambda a: a.activation(Bf1[:], Df1[:], AF.Ln), t_d2)
                    aop(ph, lambda a: a.activation(
                        Df1[:], Bf1[:], AF.Exp, scale=-1.0, bias=lnh_c[:]))
                    aop(ph, lambda a: a.activation(Bf1[:], Af1[:], AF.Ln))
                    t_rec = aop(ph, lambda a: a.activation(
                        Af1[:], Bf1[:], AF.Exp, scale=-1.0, bias=lnh_c[:]))
                    vop(ph, lambda v: v.tensor_mul(
                        srv1[:, 1:TSEQ + 1], Nf1[:], Df1[:]), t_rec, lr)
                    vb_free[p] = vop(ph, lambda v, p=p: v.tensor_mul(
                        Nf1[:], vb1[p][:], Af1[:]))
                    vop(ph, lambda v: v.tensor_add(
                        Df1[:], srv1[:, 1:TSEQ + 1], Nf1[:, ::-1]))
                    rb_free[p] = vop(ph, lambda v, p=p: v.tensor_tensor(
                        rwt1[:], Df1[:], rb1[p][:], ALU.mult), srw_st)
                    srw_st = sqop(ph, lambda a, cch=cch, c0=c0: a.dma_start(
                        out=rw_bf[cch, :, c0:c0 + TSEQ], in_=rwt1[:]),
                        rb_free[p], sem=sS_rw)

                in_use = [None, None]
                hh_rd = None
                out_free = [None] * 4
                prev_wts_use = [None, None]
                hfree = [None, None]
                wfree = [None, None]

                def sched_expert(ph, e):
                    nonlocal hh_rd
                    p = e % 2
                    ldi = qop(ph, lambda q, e=e, p=p: q.dma_start(
                        out=idwst[p][:],
                        in_=idw[e * M:(e + 1) * M, :].rearrange(
                            "(a p) w -> p a w", p=128)),
                        in_use[p], prev_wts_use[p], *r_tail, sem=sC_i[p])
                    lds = None
                    for dch in range(DCH):
                        lds = qop(ph, lambda q, e=e, p=p, dch=dch:
                                  q.dma_start_transpose(
                                      xgT[p][:, dch, :],
                                      xg[e * M:(e + 1) * M,
                                         dch * 128:(dch + 1) * 128]),
                                  in_use[p], sem=sC_x[p])
                    ldw = None
                    for dst, srcw in [(w1sb, w1_bf), (w3sb, w3_bf)]:
                        ldw = qop(ph, lambda q, dst=dst, srcw=srcw, e=e, p=p:
                                  q.dma_start(out=dst[p][:], in_=srcw[e]),
                                  in_use[p], sem=sC_w[p])
                    ldw2 = qop(ph, lambda q, e=e, p=p:
                               q.dma_start(out=w2sb[p][:], in_=w2_bf[e]),
                               hh_rd, sem=sC_w2)
                    for hs in range(4):
                        for rt_ in range(3):
                            r0 = rt_ * 512
                            rw_ = 512 if rt_ < 2 else M - 1024
                            g2 = (hs * 3 + rt_) % 2
                            b0, b1 = 2 * g2, 2 * g2 + 1

                            def wmm(t, hs=hs, r0=r0, rw_=rw_, p=p, b0=b0, b1=b1):
                                for ci in range(DCH):
                                    t.matmul(pb[b0][:, 0:rw_],
                                             w1sb[p][:, ci,
                                                     hs * 128:(hs + 1) * 128],
                                             xgT[p][:, ci, r0:r0 + rw_],
                                             start=(ci == 0),
                                             stop=(ci == DCH - 1))
                                inst = None
                                for ci in range(DCH):
                                    inst = t.matmul(
                                        pb[b1][:, 0:rw_],
                                        w3sb[p][:, ci, hs * 128:(hs + 1) * 128],
                                        xgT[p][:, ci, r0:r0 + rw_],
                                        start=(ci == 0), stop=(ci == DCH - 1))
                                return inst
                            mmtok = top(ph, wmm, lds, ldw, hfree[g2])
                            aop(ph, lambda a, b0=b0, rw_=rw_, g2=g2:
                                a.activation(silu_e[g2][:, 0:rw_],
                                             pb[b0][:, 0:rw_], AF.Silu),
                                mmtok, hfree[g2])
                            stok = aop(ph, lambda a, b1=b1, rw_=rw_, g2=g2:
                                       a.activation(h3t[g2][:, 0:rw_],
                                                    pb[b1][:, 0:rw_], AF.Copy),
                                       hfree[g2])
                            hfree[g2] = kop(
                                ph, lambda g, hs=hs, r0=r0, rw_=rw_, g2=g2:
                                g.tensor_tensor(hhE[:, hs, r0:r0 + rw_],
                                                silu_e[g2][:, 0:rw_],
                                                h3t[g2][:, 0:rw_], ALU.mult),
                                stok, hh_rd)
                    in_use[p] = s_p.cur()
                    hh_done = s_g.cur()
                    for rti in range(M // 128):
                        ob = rti % 4
                        g2 = rti % 2
                        b0, b1 = 4 + 2 * g2, 5 + 2 * g2

                        def w2mm(t, rti=rti, p=p, b0=b0, b1=b1):
                            inst = None
                            for bank, half in ((b0, 0), (b1, 1)):
                                for hs in range(4):
                                    inst = t.matmul(
                                        pb[bank][:],
                                        hhE[:, hs, rti * 128:(rti + 1) * 128],
                                        w2sb[p][:, hs,
                                                half * 512:(half + 1) * 512],
                                        start=(hs == 0), stop=(hs == 3))
                            return inst
                        mmtok = top(ph, w2mm, hh_done, wfree[g2], ldw2)
                        aop(ph, lambda a, rti=rti, ob=ob, b0=b0, p=p:
                            a.activation(outsb[ob][:, 0:512], pb[b0][:],
                                         AF.Copy, scale=idwst[p][:, rti, 1:2]),
                            mmtok, out_free[ob])
                        wfree[g2] = aop(
                            ph, lambda a, rti=rti, ob=ob, b1=b1, p=p:
                            a.activation(outsb[ob][:, 512:1024], pb[b1][:],
                                         AF.Copy, scale=idwst[p][:, rti, 1:2]))
                        evtok = s_a.cur()
                        out_free[ob] = sqop(
                            ph, lambda q, rti=rti, ob=ob, e=e:
                            q.dma_start(
                                out=out_slots[e * M + rti * 128:
                                              e * M + (rti + 1) * 128, :],
                                in_=outsb[ob][:]), evtok, sem=sC_o[ob])
                    hh_rd = s_p.cur()
                    prev_wts_use[p] = s_a.cur()

                for i in range(E):
                    sched_expert(ph, i)
                    sched_scan(ph, i)
                s_rw_done = None
                c_scat_done = None
                run(ph, "phSC")

        if upto >= 6:
            with ExitStack() as c1:
                w2bc = sbt("w2bc", [128, D], F32, c1)
                b2bc = sbt("b2bc", [128, D], F32, c1)
                rwT = [sbt(f"D_rwT{i}", [128, DCH, TSEQ], BF16, c1)
                       for i in range(2)]
                woT = sbt("D_woT", [128, DCH, D], BF16, c1)
                xnt = [sbt(f"D_xn{i}", [128, D], F32, c1) for i in range(2)]
                xrt = [sbt(f"D_xr{i}", [128, D], F32, c1) for i in range(2)]
                yrt = [sbt(f"D_yr{i}", [128, D], F32, c1) for i in range(2)]
                a2t = [sbt(f"D_a2{i}", [128, D], F32, c1) for i in range(2)]
                axy = [sbt(f"D_axy{i}", [128, D], F32, c1) for i in range(2)]
                sqd = [sbt(f"D_sq{i}", [128, D], F32, c1) for i in range(2)]
                t1d = [sbt(f"D_t1{i}", [128, D], F32, c1) for i in range(2)]
                xn2t = [sbt(f"D_xn2{i}", [128, D], BF16, c1) for i in range(2)]
                st5 = [[sbt(f"D_s{j}_{i}", [128, 1], F32, c1) for j in range(5)]
                       for i in range(2)]
                ph = Ph()
                sD_r = dsem("sD_r", 2)
                sD_w = dsem("sD_w")
                sD_x = dsem("sD_x", 2)
                sD_xr = dsem("sD_xr", 2)
                sD_y = dsem("sD_y", 2)
                sD_s1 = dsem("sD_s1", 2)
                sD_s2 = dsem("sD_s2", 2)
                wtok = gop(ph, lambda g: g.dma_start(
                    out=woT[:], in_=WoT.rearrange("(c p) o -> p c o", p=128)),
                    sem=sD_w)
                ln2_tok = qop(ph, lambda q: q.dma_start(
                    out=w2bc[:], in_=ln2w.partition_broadcast(128)), sem=sD_w)
                ln2_tok = qop(ph, lambda q: q.dma_start(
                    out=b2bc[:], in_=ln2b.partition_broadcast(128)), sem=sD_w)
                blk_use = [None, None]
                cons_xn = [None, None]
                cons_a2 = [None, None]
                cons_axy = [None, None]
                cons_x2 = [None, None]
                d1sq = [None, None]
                sc_tail = [s_a.cur(), s_g.cur(), s_v.cur()]
                for tb in range(NB):
                    pbk = tb % 2
                    t0 = tb * TSEQ
                    ldr = None
                    for cch in range(DCH):
                        ldr = qop(ph, lambda q, pbk=pbk, cch=cch, t0=t0:
                                  q.dma_start(
                                      out=rwT[pbk][:, cch, :],
                                      in_=rw_bf[cch, :, t0:t0 + TSEQ]),
                                  blk_use[pbk], *sc_tail, sem=sD_r[pbk])
                    for tt in range(16):
                        p = tt % 2
                        r0 = t0 + tt * 128
                        b0, b1 = 2 * p, 2 * p + 1
                        ldx = qop(ph, lambda q, p=p, r0=r0: q.dma_start(
                            out=xnt[p][:], in_=xn_f[r0:r0 + 128, :]), cons_xn[p],
                            sem=sD_x[p])
                        ldxr = qop(ph, lambda q, p=p, r0=r0: q.dma_start(
                            out=xrt[p][:], in_=x_in[r0:r0 + 128, :]),
                            cons_axy[p], sem=sD_xr[p])
                        ldy = qop(ph, lambda q, p=p, r0=r0: q.dma_start(
                            out=yrt[p][:], in_=y_f[r0:r0 + 128, :]),
                            cons_axy[p], sem=sD_y[p])

                        def omm(t, tt=tt, pbk=pbk, b0=b0, b1=b1):
                            inst = None
                            for bank, half in ((b0, 0), (b1, 1)):
                                for cch in range(DCH):
                                    inst = t.matmul(
                                        pb[bank][:],
                                        rwT[pbk][:, cch, tt * 128:(tt + 1) * 128],
                                        woT[:, cch, half * 512:(half + 1) * 512],
                                        start=(cch == 0), stop=(cch == DCH - 1))
                            return inst
                        mmtok = top(ph, omm, ldr, wtok, cons_a2[p],
                                    *sc_tail, s_v.cur(), s_a.cur(),
                                    q_s.cur(), q_a.cur())
                        vop(ph, lambda v, p=p, b0=b0: v.tensor_add(
                            a2t[p][:, 0:512], xnt[p][:, 0:512], pb[b0][:]),
                            mmtok, ldx)
                        t_a2 = vop(ph, lambda v, p=p, b1=b1: v.tensor_add(
                            a2t[p][:, 512:1024], xnt[p][:, 512:1024], pb[b1][:]))
                        vop(ph, lambda v, p=p: v.reduce_sum(
                            st5[p][0][:], a2t[p][:], axis=AX))
                        t_nm = vop(ph, lambda v, p=p: v.tensor_scalar_mul(
                            st5[p][1][:], st5[p][0][:], -1.0 / D), d1sq[p])
                        t_sq = aop(ph, lambda a, p=p: a.activation(
                            sqd[p][:], a2t[p][:], AF.Square, bias=st5[p][1][:],
                            accum_out=st5[p][2][:]), t_nm)
                        d1sq[p] = t_sq
                        aop(ph, lambda a, p=p: a.activation(
                            st5[p][3][:], st5[p][2][:], AF.Ln, bias=eps_c[:],
                            scale=1.0 / D))
                        t_rs = aop(ph, lambda a, p=p: a.activation(
                            st5[p][4][:], st5[p][3][:], AF.Exp, scale=-0.5))
                        vop(ph, lambda v, p=p: v.tensor_scalar(
                            t1d[p][:], a2t[p][:], st5[p][1][:], st5[p][4][:],
                            ALU.add, ALU.mult), t_rs)
                        vop(ph, lambda v, p=p: v.tensor_mul(
                            t1d[p][:], t1d[p][:], w2bc[:]), ln2_tok)
                        t_x2 = vop(ph, lambda v, p=p: v.tensor_tensor(
                            xn2t[p][:], t1d[p][:], b2bc[:], ALU.add),
                            cons_x2[p])
                        vop(ph, lambda v, p=p: v.tensor_add(
                            axy[p][:], a2t[p][:], xrt[p][:]), ldxr, ldy,
                            cons_a2[p])
                        t_axy = vop(ph, lambda v, p=p: v.tensor_add(
                            axy[p][:], axy[p][:], yrt[p][:]))
                        cons_axy[p] = t_axy
                        st1 = qop(ph, lambda q, p=p, r0=r0: q.dma_start(
                            out=a2xy_f[r0:r0 + 128, :], in_=axy[p][:]), t_axy,
                            sem=sD_s1[p])
                        st2 = qop(ph, lambda q, p=p, r0=r0: q.dma_start(
                            out=xn2_bf[r0:r0 + 128, :], in_=xn2t[p][:]), t_x2,
                            sem=sD_s2[p])
                        cons_a2[p] = st1
                        cons_xn[p] = st2
                        cons_x2[p] = st2
                    blk_use[pbk] = s_p.cur()
                d1_store = None
                run(ph, "phD1")

            with ExitStack() as c1:
                wfk = sbt("D_wfk", [128, DCH, HK], BF16, c1)
                xn2T = [sbt(f"D_x2T{i}", [128, DCH, TSEQ], BF16, c1)
                        for i in range(2)]
                relu_t = [sbt(f"D_rel{i}", [128, 512], F32, c1) for i in range(3)]
                kst = [sbt(f"D_kst{i}", [128, 512], BF16, c1) for i in range(3)]
                ph = Ph()
                s2_T = dsem("s2_T", 2)
                s2_s = dsem("s2_s", 3)
                s2_w = dsem("s2_w")
                wtok = gop(ph, lambda g: g.dma_start(
                    out=wfk[:], in_=WfkT.rearrange("(c p) o -> p c o", p=128)),
                    sem=s2_w)
                blk_use = [None, None]
                bank_free = [None] * 4
                buf_free = [None] * 3
                for tb in range(NB):
                    pbk = tb % 2
                    t0 = tb * TSEQ
                    ldt = None
                    for dch in range(DCH):
                        ldt = qop(ph, lambda q, pbk=pbk, dch=dch, t0=t0:
                                  q.dma_start_transpose(
                                      xn2T[pbk][:, dch, :],
                                      xn2_bf[t0:t0 + TSEQ,
                                             dch * 128:(dch + 1) * 128]),
                                  blk_use[pbk], sem=s2_T[pbk])
                    gi = 0
                    for hkc in range(HKCH):
                        for ts in range(4):
                            bank = gi % 4
                            bi = gi % 3
                            gi += 1

                            def kmm(t, hkc=hkc, ts=ts, pbk=pbk, bank=bank):
                                inst = None
                                for ci in range(DCH):
                                    inst = t.matmul(
                                        pb[bank][:],
                                        wfk[:, ci, hkc * 128:(hkc + 1) * 128],
                                        xn2T[pbk][:, ci, ts * 512:(ts + 1) * 512],
                                        start=(ci == 0), stop=(ci == DCH - 1))
                                return inst
                            mmtok = top(ph, kmm, ldt, wtok, bank_free[bank])
                            bank_free[bank] = vop(
                                ph, lambda v, bank=bank, bi=bi:
                                v.tensor_scalar_max(relu_t[bi][:], pb[bank][:],
                                                    0.0),
                                mmtok, buf_free[bi])
                            t_sqr = aop(ph, lambda a, bi=bi: a.activation(
                                kst[bi][:], relu_t[bi][:], AF.Square),
                                bank_free[bank])
                            buf_free[bi] = qop(
                                ph, lambda q, hkc=hkc, ts=ts, t0=t0, bi=bi:
                                q.dma_start(
                                    out=kk_bf[hkc, :,
                                              t0 + ts * 512:t0 + ts * 512 + 512],
                                    in_=kst[bi][:]), t_sqr, sem=s2_s[bi])
                    blk_use[pbk] = s_p.cur()
                d2a_store = None
                run(ph, "phD2a")

            with ExitStack() as c1:
                wfv = sbt("D_wfv", [128, HKCH, D], BF16, c1)
                wfr = sbt("D_wfr", [128, DCH, D], BF16, c1)
                kks = [sbt(f"D_kks{i}", [128, HKCH, 512], BF16, c1)
                       for i in range(2)]
                xn2Th = [sbt(f"D_x2h{i}", [128, DCH, 512], BF16, c1)
                         for i in range(2)]
                a2l = [sbt(f"D_a2l{i}", [128, D], F32, c1) for i in range(2)]
                sg = [sbt(f"D_sg{i}", [128, D], F32, c1) for i in range(2)]
                ot = [sbt(f"D_ot{i}", [128, D], F32, c1) for i in range(2)]
                gbuf = [sbt(f"D_gb{i}", [128, D], BF16, c1) for i in range(8)]
                stl = [sbt(f"D_stl{i}", [128, 4], U32, c1) for i in range(2)]
                ph = Ph()
                s3_k = dsem("s3_k", 2)
                s3_T = dsem("s3_T", 2)
                s3_a = dsem("s3_a", 2)
                s3_s = dsem("s3_s", 2)
                s3_g = dsem("s3_g", 2)
                s3_o = dsem("s3_o", 2)
                s3_w = dsem("s3_w")
                wtok = gop(ph, lambda g: g.dma_start(
                    out=wfv[:], in_=WfvT.rearrange("(c p) o -> p c o", p=128)),
                    sem=s3_w)
                wtok = gop(ph, lambda g: g.dma_start(
                    out=wfr[:], in_=WfrT.rearrange("(c p) o -> p c o", p=128)),
                    sem=s3_w)
                blk_use = [None, None]
                cons = [None, None]
                out_st = [None, None]
                gts_tok = [None, None]
                for hb in range(16):
                    pbk = hb % 2
                    h0 = hb * 512
                    ldk = None
                    for hkc in range(HKCH):
                        ldk = qop(ph, lambda q, pbk=pbk, hkc=hkc, h0=h0:
                                  q.dma_start(
                                      out=kks[pbk][:, hkc, :],
                                      in_=kk_bf[hkc, :, h0:h0 + 512]),
                                  blk_use[pbk], sem=s3_k[pbk])
                    ldt = None
                    for dch in range(DCH):
                        ldt = qop(ph, lambda q, pbk=pbk, dch=dch, h0=h0:
                                  q.dma_start_transpose(
                                      xn2Th[pbk][:, dch, :],
                                      xn2_bf[h0:h0 + 512,
                                             dch * 128:(dch + 1) * 128]),
                                  blk_use[pbk], sem=s3_T[pbk])
                    for tt in range(4):
                        p = tt % 2
                        r0 = h0 + tt * 128
                        fb0, fb1 = 2 * p, 2 * p + 1
                        rb0, rb1 = 4 + 2 * p, 5 + 2 * p
                        ldx = qop(ph, lambda q, p=p, r0=r0: q.dma_start(
                            out=a2l[p][:], in_=a2xy_f[r0:r0 + 128, :]), cons[p],
                            sem=s3_a[p])
                        lds = qop(ph, lambda q, p=p, r0=r0: q.dma_start(
                            out=stl[p][:], in_=slots_tok[r0:r0 + 128, :]),
                            cons[p], gts_tok[p], sem=s3_s[p])
                        for j in range(4):
                            gop(ph, lambda g, p=p, j=j:
                                g.indirect_dma_start(
                                    out=gbuf[4 * p + j][:], out_offset=None,
                                    in_=out_slots,
                                    in_offset=IndirectOffsetOnAxis(
                                        stl[p][:, j:j + 1], 0)),
                                lds, cons[p], sem=s3_g[p])
                        gts_tok[p] = s3_g[p].cur()

                        def fmm(t, tt=tt, pbk=pbk, fb0=fb0, fb1=fb1,
                                rb0=rb0, rb1=rb1):
                            for bank, half in ((fb0, 0), (fb1, 1)):
                                for hkc in range(HKCH):
                                    t.matmul(
                                        pb[bank][:],
                                        kks[pbk][:, hkc,
                                                 tt * 128:(tt + 1) * 128],
                                        wfv[:, hkc, half * 512:(half + 1) * 512],
                                        start=(hkc == 0),
                                        stop=(hkc == HKCH - 1))
                            inst = None
                            for bank, half in ((rb0, 0), (rb1, 1)):
                                for ci in range(DCH):
                                    inst = t.matmul(
                                        pb[bank][:],
                                        xn2Th[pbk][:, ci,
                                                   tt * 128:(tt + 1) * 128],
                                        wfr[:, ci, half * 512:(half + 1) * 512],
                                        start=(ci == 0), stop=(ci == DCH - 1))
                            return inst
                        mmtok = top(ph, fmm, ldk, ldt, wtok, cons[p])
                        aop(ph, lambda a, p=p, rb0=rb0: a.activation(
                            sg[p][:, 0:512], pb[rb0][:], AF.Sigmoid), mmtok)
                        t_sg = aop(ph, lambda a, p=p, rb1=rb1: a.activation(
                            sg[p][:, 512:1024], pb[rb1][:], AF.Sigmoid))
                        vop(ph, lambda v, p=p, fb0=fb0: v.tensor_tensor(
                            ot[p][:, 0:512], sg[p][:, 0:512], pb[fb0][:],
                            ALU.mult), t_sg, mmtok, out_st[p])
                        vop(ph, lambda v, p=p, fb1=fb1: v.tensor_tensor(
                            ot[p][:, 512:1024], sg[p][:, 512:1024], pb[fb1][:],
                            ALU.mult))
                        vop(ph, lambda v, p=p: v.tensor_add(
                            ot[p][:], ot[p][:], a2l[p][:]), ldx)
                        for j in range(4):
                            vop(ph, lambda v, p=p, j=j: v.tensor_add(
                                ot[p][:], ot[p][:], gbuf[4 * p + j][:]),
                                gts_tok[p])
                        t_o = s_v.cur()
                        cons[p] = s_v.cur()
                        out_st[p] = qop(ph, lambda q, p=p, r0=r0: q.dma_start(
                            out=out_d[r0:r0 + 128, :], in_=ot[p][:]), t_o,
                            sem=s3_o[p])
                    blk_use[pbk] = s_p.cur()
                run(ph, "phD2b")

        ph = Ph()
        emit(ph, "sync", lambda q: q.nop(), waits=allcur())
        run(ph, "fin")

    return nc


_cache = {}


def _get_prog(upto=99, debug=False):
    key = (upto, debug)
    if key not in _cache:
        _cache[key] = build(upto=upto, debug=debug)
    return _cache[key]


def _prep_inputs(inputs):
    f = np.ascontiguousarray

    def g(name):
        return np.asarray(inputs[name])

    x = g("L0_attn_x").reshape(32, 2048, 1024)
    base = {
        "ln1w": f(g("L0_attn_ln1_w").reshape(1, D)),
        "ln1b": f(g("L0_attn_ln1_b").reshape(1, D)),
        "ln2w": f(g("L0_attn_ln2_w").reshape(1, D)),
        "ln2b": f(g("L0_attn_ln2_b").reshape(1, D)),
        "WrT": f(g("L0_attn_Wr").T),
        "WkT": f(g("L0_attn_Wk").T),
        "WvT": f(g("L0_attn_Wv").T),
        "WoT": f(g("L0_attn_Wo").T),
        "WfkT": f(g("L0_attn_Wfk").T),
        "WfvT": f(g("L0_attn_Wfv").T),
        "WfrT": f(g("L0_attn_Wfr").T),
        "Wg": f(g("L0_ffn_Wg")),
        "Ws1": f(g("L0_ffn_Ws1")),
        "Ws3": f(g("L0_ffn_Ws3")),
        "Ws2": f(g("L0_ffn_Ws2")),
        "W1": f(g("L0_ffn_W1")),
        "W3": f(g("L0_ffn_W3")),
        "W2": f(g("L0_ffn_W2")),
        "decay": f(g("L0_attn_decay").reshape(DCH, 128).T),
        "u": f(g("L0_attn_u").reshape(DCH, 128).T),
        "tri": np.triu(np.ones((128, 128), np.float32), 1),
        "onesc": np.ones((128, 1), np.float32),
        "onesr": np.ones((1, 128), np.float32),
        "iotaEM_i": np.broadcast_to((np.arange(E) * M).astype(np.float32),
                                    (128, E)).copy(),
        "iotap_i": np.arange(128, dtype=np.float32).reshape(128, 1),
    }
    maps = []
    for c in range(NCORES):
        m = dict(base)
        m["x"] = f(x[c * NSEQ:(c + 1) * NSEQ].reshape(TC, D))
        maps.append(m)
    return maps


def kernel(**inputs):
    nc = _get_prog()
    in_maps = _prep_inputs(inputs)
    res = run_bass_kernel_spmd(nc, in_maps, core_ids=list(range(NCORES)))
    out = np.stack([np.asarray(r["out"]).reshape(NSEQ, 2048, D)
                    for r in res.results])
    return out.reshape(32, 2048, D)


# version 30
# speedup vs baseline: 1.1926x; 1.0082x over previous
import numpy as np
from contextlib import ExitStack

import concourse.bass as bass
import concourse.mybir as mybir
from concourse.bass import IndirectOffsetOnAxis
from concourse.bass_utils import run_bass_kernel_spmd

F32 = mybir.dt.float32
BF16 = mybir.dt.bfloat16
U32 = mybir.dt.uint32
AF = mybir.ActivationFunctionType
ALU = mybir.AluOpType
AX = mybir.AxisListType.X

NCORES = 8
TSEQ = 2048
NSEQ = 4
TC = NSEQ * TSEQ
D = 1024
DCH = 8
E = 32
H = 512
HK = 4096
HKCH = 32
M = 1280
NT = TC // 128
NB = 4
EPS = 1e-5
LN_HALF = float(np.log(0.5))

ENGINES = ("sync", "vector", "scalar", "tensor", "gpsimd")


class Sem:
    __slots__ = ("h", "n")

    def __init__(self, h):
        self.h = h
        self.n = 0

    def cur(self):
        return (self, self.n)


def _merge_waits(waits):
    best = {}
    for w in waits:
        if w is None:
            continue
        s, v = w
        if v <= 0:
            continue
        if best.get(s.h, -1) < v:
            best[s.h] = v
    return list(best.items())


class Ph:
    def __init__(self):
        self.q = {e: [] for e in ENGINES}

    def flush(self, block):
        for e in ENGINES:
            fns = self.q[e]
            if fns:
                def runner(X, fns=fns):
                    for f in fns:
                        f(X)
                getattr(block, e)(runner)


def build(upto=99, debug=False):
    nc = bass.Bass("TRN2", target_bir_lowering=False)

    def din(name, shape, dtype=F32):
        return nc.dram_tensor(name, shape, dtype, kind="ExternalInput").ap()

    x_in = din("x", [TC, D])
    ln1w = din("ln1w", [1, D]);  ln1b = din("ln1b", [1, D])
    ln2w = din("ln2w", [1, D]);  ln2b = din("ln2b", [1, D])
    WrT = din("WrT", [D, D]);  WkT = din("WkT", [D, D])
    WvT = din("WvT", [D, D]);  WoT = din("WoT", [D, D])
    WfkT = din("WfkT", [D, HK]); WfvT = din("WfvT", [HK, D]); WfrT = din("WfrT", [D, D])
    Wg = din("Wg", [D, E])
    Ws1 = din("Ws1", [D, D]); Ws3 = din("Ws3", [D, D]); Ws2 = din("Ws2", [D, D])
    W1 = din("W1", [E, D, H]); W3 = din("W3", [E, D, H]); W2 = din("W2", [E, H, D])
    decay = din("decay", [128, DCH]); u_in = din("u", [128, DCH])
    tri_in = din("tri", [128, 128])
    onesc = din("onesc", [128, 1]); onesr = din("onesr", [1, 128])
    iotaEM_in = din("iotaEM_i", [128, E]); iotap_in = din("iotap_i", [128, 1])

    sk = "ExternalOutput" if debug else "Internal"

    def dsc(name, shape, dtype=F32):
        return nc.dram_tensor(name, shape, dtype, kind=sk).ap()

    xn_f = dsc("xn_f", [TC, D])
    xn_bf = dsc("xn_bf", [TC, D], BF16)
    xna_bf = dsc("xna_bf", [TC, D], BF16)
    k_f = dsc("k_f", [DCH, 128, TC])
    v_f = dsc("v_f", [DCH, 128, TC])
    r_f = dsc("r_f", [DCH, 128, TC], BF16)
    rw_bf = dsc("rw_bf", [DCH, 128, TC], BF16)
    xg = dsc("xg_s", [E * M, D], BF16)
    idw = dsc("idw_s", [E * M, 2])
    y_f = dsc("y_f", [TC + 128, D])
    a2_f = dsc("a2_f", [TC, D])
    xn2_bf = dsc("xn2_bf", [TC, D], BF16)
    kk_bf = dsc("kk_bf", [HKCH, 128, TC], BF16)
    out_slots = dsc("oslot", [E * M, D], BF16)
    slots_tok = dsc("stok", [TC, 4], U32)
    a2xy_f = dsc("a2xy_f", [TC, D])
    w1_bf = dsc("w1_bf", [E, 128, DCH, H], BF16)
    w3_bf = dsc("w3_bf", [E, 128, DCH, H], BF16)
    w2_bf = dsc("w2_bf", [E, 128, 4, D], BF16)
    out_d = nc.dram_tensor("out", [TC, D], F32, kind="ExternalOutput").ap()

    with ExitStack() as ctx0:
        def sbt(name, shape, dtype=F32, stack=None):
            return (stack or ctx0).enter_context(nc.sbuf_tensor(name, shape, dtype))

        d_ex = sbt("d_ex", [128, DCH]); eu = sbt("eu", [128, DCH])
        tmp8 = sbt("tmp8", [128, DCH])
        tri_f = sbt("tri_f", [128, 128])
        ones_c = sbt("ones_c", [128, 1]); ones_r = sbt("ones_r", [1, 128])
        iotaEM = sbt("iotaEM", [128, E]); clampM = sbt("clampM", [128, E])
        iota_p = sbt("iota_p", [128, 1])
        eps_c = sbt("eps_c", [128, 1]); lnh_c = sbt("lnh_c", [128, 1])

        pb = [ctx0.enter_context(nc.psum_tensor(f"pb{i}", [128, 512], F32))
              for i in range(8)]

        q_s = Sem(ctx0.enter_context(nc.semaphore("q_s")))
        q_g = Sem(ctx0.enter_context(nc.semaphore("q_g")))
        s_v = Sem(ctx0.enter_context(nc.semaphore("s_v")))
        s_a = Sem(ctx0.enter_context(nc.semaphore("s_a")))
        s_p = Sem(ctx0.enter_context(nc.semaphore("s_p")))
        q_a = Sem(ctx0.enter_context(nc.semaphore("q_a")))
        s_g = Sem(ctx0.enter_context(nc.semaphore("s_g")))
        allsems = [q_s, q_g, s_v, s_a, s_p, q_a, s_g]

        def dsem(name, n=1):
            out = []
            for i in range(n):
                s = Sem(ctx0.enter_context(nc.semaphore(f"{name}{i}")))
                allsems.append(s)
                out.append(s)
            return out if n > 1 else out[0]

        def emit(ph, eng, build_fn, waits=(), inc=None, inck=1):
            wl = _merge_waits(waits)
            if inc is not None:
                inc.n += inck
                ih = inc.h
            else:
                ih = None

            def fn(X, wl=wl, build_fn=build_fn, ih=ih, inck=inck):
                for h, v in wl:
                    X.wait_ge(h, v)
                inst = build_fn(X)
                if ih is not None:
                    inst.then_inc(ih, inck)
            ph.q[eng].append(fn)

        def vop(ph, bf_, *extra):
            emit(ph, "vector", bf_, waits=[s_v.cur(), *extra], inc=s_v)
            return s_v.cur()

        def aop(ph, bf_, *extra):
            emit(ph, "scalar", bf_, waits=[s_a.cur(), *extra], inc=s_a)
            return s_a.cur()

        def top(ph, bf_, *extra):
            emit(ph, "tensor", bf_, waits=list(extra), inc=s_p)
            return s_p.cur()

        def qop(ph, bf_, *extra, sem=None):
            s = sem or q_s
            emit(ph, "sync", bf_, waits=list(extra), inc=s, inck=16)
            return s.cur()

        def gop(ph, bf_, *extra, sem=None):
            s = sem or q_g
            emit(ph, "gpsimd", bf_, waits=list(extra), inc=s, inck=16)
            return s.cur()

        def sqop(ph, bf_, *extra, sem=None):
            s = sem or q_a
            emit(ph, "scalar", bf_, waits=list(extra), inc=s, inck=16)
            return s.cur()

        def kop(ph, bf_, *extra):
            emit(ph, "gpsimd", bf_, waits=[s_g.cur(), *extra], inc=s_g)
            return s_g.cur()

        def allcur():
            return [s.cur() for s in allsems]

        class Precast:
            def __init__(self, stage, src_w, dst_w, rrs, lsems, ssems):
                self.stage = stage
                self.src_w = src_w
                self.dst_w = dst_w
                self.rrs = rrs
                self.i = 0
                self.stok = [None, None]
                self.lsems = lsems
                self.ssems = ssems

            def step(self, ph):
                if self.i >= E:
                    return
                e = self.i
                pp = e % 2
                lt = gop(ph, lambda g, e=e, pp=pp, s=self:
                         g.dma_start(out=s.stage[pp][:],
                                     in_=s.src_w[e].rearrange(s.rrs, p=128)),
                         self.stok[pp], sem=self.lsems[pp])
                self.stok[pp] = qop(ph, lambda q, e=e, pp=pp, s=self:
                                    q.dma_start(out=s.dst_w[e],
                                                in_=s.stage[pp][:]), lt,
                                    sem=self.ssems[pp])
                self.i += 1

            def finish(self, ph):
                while self.i < E:
                    self.step(ph)

        def run(ph, name="ph"):
            with nc.named_scope(name), nc.Block() as block:
                ph.flush(block)
                waits = _merge_waits(allcur())
                for e in ENGINES:
                    def bar(X, waits=waits):
                        for h, v in waits:
                            X.wait_ge(h, v)
                    getattr(block, e)(bar)

        prep_stack = ExitStack()
        sent = sbt("sent", [128, (E * M * 2) // 128], F32, prep_stack)
        ph = Ph()
        for dst, src in [(d_ex, decay), (eu, u_in), (tri_f, tri_in),
                         (ones_c, onesc), (ones_r, onesr),
                         (iotaEM, iotaEM_in), (iota_p, iotap_in)]:
            qop(ph, lambda q, dst=dst, src=src: q.dma_start(out=dst[:], in_=src))
        ldp = q_s.cur()
        aop(ph, lambda a: a.activation(tmp8[:], d_ex[:], AF.Exp), ldp)
        aop(ph, lambda a: a.activation(d_ex[:], tmp8[:], AF.Exp, scale=-1.0))
        t_eu = aop(ph, lambda a: a.activation(tmp8[:], eu[:], AF.Exp))
        vop(ph, lambda v: v.tensor_copy(eu[:], tmp8[:]), t_eu)
        vop(ph, lambda v: v.tensor_scalar_add(clampM[:], iotaEM[:], float(M - 1)), ldp)
        vop(ph, lambda v: v.memset(eps_c[:], EPS))
        vop(ph, lambda v: v.memset(lnh_c[:], LN_HALF))
        t_sent = vop(ph, lambda v: v.memset(sent[:], float(TC)))
        qop(ph, lambda q: q.dma_start(
            out=idw.rearrange("(p a) w -> p (a w)", p=128), in_=sent[:]), t_sent)
        run(ph, "prep")
        prep_stack.close()

        a_store_q = None
        a_store_g = None
        if upto >= 1:
            with ExitStack() as c1:
                w1bc = sbt("w1bc", [128, D], F32, c1)
                b1bc = sbt("b1bc", [128, D], F32, c1)
                pcs = [sbt(f"A_pc{i}", [128, DCH, H], BF16, c1) for i in range(2)]
                xa = [sbt(f"A_x{i}", [128, D], F32, c1) for i in range(2)]
                sq = [sbt(f"A_sq{i}", [128, D], F32, c1) for i in range(2)]
                xn_t = [sbt(f"A_xn{i}", [128, D], F32, c1) for i in range(2)]
                t1 = [sbt(f"A_t1{i}", [128, D], F32, c1) for i in range(2)]
                xna_t = [sbt(f"A_xna{i}", [128, D], BF16, c1) for i in range(2)]
                xnb_t = [sbt(f"A_xnb{i}", [128, D], BF16, c1) for i in range(2)]
                st5 = [[sbt(f"A_s{j}_{i}", [128, 1], F32, c1) for j in range(5)]
                       for i in range(2)]
                ph = Ph()
                lnw_tok = qop(ph, lambda q: q.dma_start(
                    out=w1bc[:], in_=ln1w.partition_broadcast(128)))
                lnw_tok = qop(ph, lambda q: q.dma_start(
                    out=b1bc[:], in_=ln1b.partition_broadcast(128)))
                sA_x = dsem("sA_x", 2)
                sA_st = dsem("sA_st", 2)
                sA_st2 = dsem("sA_st2", 2)
                sA_pcl = dsem("sA_pcl", 2)
                sA_pcs = dsem("sA_pcs", 2)
                pc = Precast(pcs, W1, w1_bf, "(c p) h -> p c h", sA_pcl, sA_pcs)
                cons_xa = [None, None]
                cons_xn = [[], []]
                cons_nm = [None, None]
                cons_xna = [None, None]
                cons_xn2b = [None, None]
                for t in range(NT):
                    p = t % 2
                    r0 = t * 128
                    ld = qop(ph, lambda q, p=p, r0=r0: q.dma_start(
                        out=xa[p][:], in_=x_in[r0:r0 + 128, :]), cons_xa[p],
                        sem=sA_x[p])
                    vop(ph, lambda v, p=p: v.reduce_sum(
                        st5[p][0][:], xa[p][:], axis=AX), ld)
                    t_nm = vop(ph, lambda v, p=p: v.tensor_scalar_mul(
                        st5[p][1][:], st5[p][0][:], -1.0 / D), cons_nm[p])
                    t_sq = aop(ph, lambda a, p=p: a.activation(
                        sq[p][:], xa[p][:], AF.Square, bias=st5[p][1][:],
                        accum_out=st5[p][2][:]), t_nm, ld)
                    cons_xa[p] = t_sq
                    aop(ph, lambda a, p=p: a.activation(
                        st5[p][3][:], st5[p][2][:], AF.Ln, bias=eps_c[:],
                        scale=1.0 / D))
                    t_rs = aop(ph, lambda a, p=p: a.activation(
                        st5[p][4][:], st5[p][3][:], AF.Exp, scale=-0.5))
                    t_xn = vop(ph, lambda v, p=p: v.tensor_scalar(
                        xn_t[p][:], xa[p][:], st5[p][1][:], st5[p][4][:],
                        ALU.add, ALU.mult), t_rs, *cons_xn[p])
                    vop(ph, lambda v, p=p: v.reduce_sum(
                        st5[p][0][:], xn_t[p][:], axis=AX))
                    t_nm2 = vop(ph, lambda v, p=p: v.tensor_scalar_mul(
                        st5[p][1][:], st5[p][0][:], -1.0 / D), t_sq)
                    t_sq2 = aop(ph, lambda a, p=p: a.activation(
                        sq[p][:], xn_t[p][:], AF.Square, bias=st5[p][1][:],
                        accum_out=st5[p][2][:]), t_nm2, t_xn)
                    aop(ph, lambda a, p=p: a.activation(
                        st5[p][3][:], st5[p][2][:], AF.Ln, bias=eps_c[:],
                        scale=1.0 / D))
                    t_rs2 = aop(ph, lambda a, p=p: a.activation(
                        st5[p][4][:], st5[p][3][:], AF.Exp, scale=-0.5))
                    vop(ph, lambda v, p=p: v.tensor_scalar(
                        t1[p][:], xn_t[p][:], st5[p][1][:], st5[p][4][:],
                        ALU.add, ALU.mult), t_rs2)
                    vop(ph, lambda v, p=p: v.tensor_mul(t1[p][:], t1[p][:],
                                                        w1bc[:]), lnw_tok)
                    t_xna = vop(ph, lambda v, p=p: v.tensor_tensor(
                        xna_t[p][:], t1[p][:], b1bc[:], ALU.add), cons_xna[p])
                    t_xnb = vop(ph, lambda v, p=p: v.tensor_copy(
                        xnb_t[p][:], xn_t[p][:]), cons_xn2b[p])
                    st1 = qop(ph, lambda q, p=p, r0=r0: q.dma_start(
                        out=xn_f[r0:r0 + 128, :], in_=xn_t[p][:]), t_xn)
                    st2 = qop(ph, lambda q, p=p, r0=r0: q.dma_start(
                        out=xna_bf[r0:r0 + 128, :], in_=xna_t[p][:]), t_xna)
                    st3 = qop(ph, lambda q, p=p, r0=r0: q.dma_start(
                        out=xn_bf[r0:r0 + 128, :], in_=xnb_t[p][:]), t_xnb)
                    cons_xn[p] = [st1, t_sq2]
                    cons_xn2b[p] = st3
                    cons_nm[p] = t_sq2
                    cons_xna[p] = st2
                    if t % 2 == 1:
                        pc.step(ph)
                pc.finish(ph)
                a_store_q = None
                a_store_g = None
                run(ph, "phA")

        b_store = None
        if upto >= 2:
            with ExitStack() as c1:
                wt = {nm: sbt(f"B_W{nm}", [128, DCH, D], BF16, c1)
                      for nm in ("r", "k", "v")}
                xnaT = [sbt(f"B_xnaT{i}", [128, DCH, TSEQ], BF16, c1)
                        for i in range(2)]
                ev = [sbt(f"B_ev{i}", [128, 512], F32, c1) for i in range(4)]
                evb = [sbt(f"B_evb{i}", [128, 512], BF16, c1) for i in range(4)]
                pcs = [sbt(f"B_pc{i}", [128, DCH, H], BF16, c1) for i in range(2)]
                ph = Ph()
                sB_xn = dsem("sB_xn", 2)
                sB_ev = dsem("sB_ev", 4)
                sB_w = dsem("sB_w")
                sB_pcl = dsem("sB_pcl", 2)
                sB_pcs = dsem("sB_pcs", 2)
                wtok = None
                for nm, src in [("r", WrT), ("k", WkT), ("v", WvT)]:
                    wtok = gop(ph, lambda g, nm=nm, src=src: g.dma_start(
                        out=wt[nm][:],
                        in_=src.rearrange("(c p) o -> p c o", p=128)),
                        a_store_g, sem=sB_w)
                pc = Precast(pcs, W3, w3_bf, "(c p) h -> p c h", sB_pcl, sB_pcs)
                bank_free = [None] * 4
                ev_free = [None] * 4
                blk_mm = [None, None]
                ld_tok = [None, None]
                for tb in range(NB):
                    pbk = tb % 2
                    t0 = tb * TSEQ
                    for dch in range(DCH):
                        ld_tok[pbk] = qop(ph, lambda q, pbk=pbk, dch=dch, t0=t0:
                                          q.dma_start_transpose(
                                              xnaT[pbk][:, dch, :],
                                              xna_bf[t0:t0 + TSEQ,
                                                     dch * 128:(dch + 1) * 128]),
                                          a_store_q, blk_mm[pbk],
                                          sem=sB_xn[pbk])
                    gi = 0
                    for nm in ("k", "v", "r"):
                        dst = {"k": k_f, "v": v_f, "r": r_f}[nm]
                        for co in range(DCH):
                            for ts in range(4):
                                bank = gi % 4
                                ei = gi % 4
                                gi += 1

                                def mmg(t, nm=nm, co=co, ts=ts, bank=bank, pbk=pbk):
                                    inst = None
                                    for ci in range(DCH):
                                        inst = t.matmul(
                                            pb[bank][:],
                                            wt[nm][:, ci, co * 128:(co + 1) * 128],
                                            xnaT[pbk][:, ci,
                                                      ts * 512:(ts + 1) * 512],
                                            start=(ci == 0), stop=(ci == DCH - 1))
                                    return inst
                                mmtok = top(ph, mmg, ld_tok[pbk], wtok,
                                            bank_free[bank])
                                if nm == "r":
                                    etok = aop(ph, lambda a, bank=bank, ei=ei:
                                               a.activation(evb[ei][:], pb[bank][:],
                                                            AF.Sigmoid),
                                               mmtok, ev_free[ei])
                                    ebuf = evb[ei]
                                else:
                                    etok = vop(ph, lambda v, bank=bank, ei=ei:
                                               v.tensor_copy(ev[ei][:],
                                                             pb[bank][:]),
                                               mmtok, ev_free[ei])
                                    ebuf = ev[ei]
                                bank_free[bank] = etok
                                ev_free[ei] = qop(
                                    ph, lambda q, co=co, ts=ts, t0=t0,
                                    dst=dst, ebuf=ebuf:
                                    q.dma_start(
                                        out=dst[co, :,
                                                t0 + ts * 512:t0 + ts * 512 + 512],
                                        in_=ebuf[:]), etok, sem=sB_ev[ei])
                                if gi % 3 == 0:
                                    pc.step(ph)
                    blk_mm[pbk] = s_p.cur()
                pc.finish(ph)
                b_store = None
                run(ph, "phB")

        r_scat_done = None
        r_y_done = None
        if upto >= 3:
            with ExitStack() as c1:
                ws1 = sbt("R_Ws1", [128, DCH, D], BF16, c1)
                ws3 = sbt("R_Ws3", [128, DCH, D], BF16, c1)
                ws2 = sbt("R_Ws2", [128, DCH, D], BF16, c1)
                wg = sbt("R_Wg", [128, DCH, E], BF16, c1)
                xnT = [sbt("R_xnT0", [128, DCH, TSEQ], BF16, c1)] * 2
                hh = sbt("R_hh", [128, DCH, TSEQ], BF16, c1)
                silu_t = [sbt(f"R_silu{i}", [128, 512], F32, c1) for i in range(2)]
                zt = [sbt(f"R_z{i}", [128, D], F32, c1) for i in range(2)]
                mask_all = sbt("R_mask", [128, NT * E], F32, c1)
                pref_all = sbt("R_pref", [128, NT * E], F32, c1)
                sc_all = sbt("R_sc", [128, NT * E], F32, c1)
                mcum_all = sbt("R_mcum", [128, NT * E], F32, c1)
                toff_all = sbt("R_toff", [128, NT * E], F32, c1)
                cnt_a = sbt("R_cnta", [1, NT * E], F32, c1)
                cnt_b = sbt("R_cntb", [1, NT * E], F32, c1)
                rt = {}
                for nm, w_ in [("m", 1), ("nm", 1), ("sc", E), ("ss", 1),
                               ("rs", 1), ("m8", 8), ("gt", E), ("eq", E),
                               ("cg", 1), ("nd", 1), ("ca", E), ("cb", E),
                               ("sel", E)]:
                    rt[nm] = sbt(f"Rt_{nm}", [128, w_], F32, c1)
                slotm = sbt("R_slm", [128, E], F32, c1)
                t_oh = sbt("R_oh", [128, E], F32, c1)
                t_tmpE = sbt("R_tmpE", [128, E], F32, c1)
                slotf = [sbt(f"R_slf{i}", [128, 4], F32, c1) for i in range(2)]
                slotu = [sbt(f"R_slu{i}", [128, 4], U32, c1) for i in range(2)]
                packb = [sbt(f"R_pack{i}", [128, 8], F32, c1) for i in range(2)]
                idcol = sbt("R_idc", [128, 1], F32, c1)
                xn_sb = [sbt(f"R_xnsb{i}", [128, D], BF16, c1) for i in range(2)]
                pcs = [sbt(f"R_pc{i}", [128, 4, D], BF16, c1) for i in range(2)]

                ph = Ph()
                sR_xn = dsem("sR_xn")
                sR_w = dsem("sR_w")
                sR_z = dsem("sR_z", 2)
                sR_xs = dsem("sR_xs", 2)
                sR_ss = dsem("sR_ss", 2)
                sR_sc = dsem("sR_sc", 2)
                sR_pcl = dsem("sR_pcl", 2)
                sR_pcs = dsem("sR_pcs", 2)
                b_tail = [s_v.cur(), s_a.cur(), b_store]
                wtok = None
                for dst, src in [(ws1, Ws1), (ws3, Ws3), (ws2, Ws2), (wg, Wg)]:
                    wtok = gop(ph, lambda g, dst=dst, src=src: g.dma_start(
                        out=dst[:], in_=src.rearrange("(c p) o -> p c o", p=128)),
                        a_store_g, *b_tail, sem=sR_w)
                pc = Precast(pcs, W2, w2_bf, "(c p) d -> p c d", sR_pcl, sR_pcs)

                def cumsum_freeE(ph, src_ap, dst_ap):
                    cur = src_ap
                    lvls = [1, 2, 4, 8, 16]
                    for li, dd in enumerate(lvls):
                        nxt = (rt["ca"], rt["cb"])[li % 2][:] \
                            if li < len(lvls) - 1 else dst_ap
                        vop(ph, lambda v, nxt=nxt, cur=cur, dd=dd:
                            v.tensor_copy(nxt[:, 0:dd], cur[:, 0:dd]))
                        vop(ph, lambda v, nxt=nxt, cur=cur, dd=dd:
                            v.tensor_add(nxt[:, dd:E], cur[:, dd:E],
                                         cur[:, 0:E - dd]))
                        cur = nxt

                blk_use = [None, None]
                ld_tok = [None, None]
                gate_bank_free = [None, None]
                tri_bank_free = None
                for tb in range(NB):
                    pbk = tb % 2
                    t0 = tb * TSEQ
                    for dch in range(DCH):
                        ld_tok[pbk] = qop(ph, lambda q, pbk=pbk, dch=dch, t0=t0:
                                          q.dma_start_transpose(
                                              xnT[pbk][:, dch, :],
                                              xn_bf[t0:t0 + TSEQ,
                                                    dch * 128:(dch + 1) * 128]),
                                          a_store_g, a_store_q, blk_use[pbk],
                                          *b_tail, sem=sR_xn)
                    for tt in range(16):
                        tile = tb * 16 + tt
                        cs = tile * E
                        gb = 6 + (tt % 2)

                        def gmm(t, tt=tt, pbk=pbk, gb=gb):
                            inst = None
                            for ci in range(DCH):
                                inst = t.matmul(
                                    pb[gb][:, 0:E],
                                    xnT[pbk][:, ci, tt * 128:(tt + 1) * 128],
                                    wg[:, ci, :],
                                    start=(ci == 0), stop=(ci == DCH - 1))
                            return inst
                        mmtok = top(ph, gmm, ld_tok[pbk], wtok,
                                    gate_bank_free[tt % 2])
                        vop(ph, lambda v, gb=gb: v.reduce_max(
                            rt["m"][:], pb[gb][:, 0:E], axis=AX), mmtok)
                        t_nm = vop(ph, lambda v: v.tensor_scalar_mul(
                            rt["nm"][:], rt["m"][:], -1.0))
                        t_exp = aop(ph, lambda a, gb=gb: a.activation(
                            rt["sc"][:], pb[gb][:, 0:E], AF.Exp,
                            bias=rt["nm"][:], accum_out=rt["ss"][:]), t_nm)
                        gate_bank_free[tt % 2] = t_exp
                        vop(ph, lambda v: v.reciprocal(rt["rs"][:], rt["ss"][:]),
                            t_exp)
                        vop(ph, lambda v, cs=cs: v.tensor_scalar_mul(
                            sc_all[:, cs:cs + E], rt["sc"][:], rt["rs"][:]))
                        vop(ph, lambda v, cs=cs: v.max(out=rt["m8"][:],
                                                       in_=sc_all[:, cs:cs + E]))
                        vop(ph, lambda v, cs=cs: v.tensor_single_scalar(
                            rt["gt"][:], sc_all[:, cs:cs + E], rt["m8"][:, 3:4],
                            ALU.is_gt))
                        vop(ph, lambda v, cs=cs: v.tensor_single_scalar(
                            rt["eq"][:], sc_all[:, cs:cs + E], rt["m8"][:, 3:4],
                            ALU.is_equal))
                        vop(ph, lambda v: v.reduce_sum(rt["cg"][:], rt["gt"][:],
                                                       axis=AX))
                        vop(ph, lambda v: v.tensor_scalar(
                            rt["nd"][:], rt["cg"][:], 4.0, -1.0,
                            ALU.subtract, ALU.mult))
                        cumsum_freeE(ph, rt["eq"][:], rt["sel"][:])
                        vop(ph, lambda v: v.tensor_single_scalar(
                            rt["sel"][:], rt["sel"][:], rt["nd"][:], ALU.is_le))
                        vop(ph, lambda v: v.tensor_mul(rt["sel"][:], rt["sel"][:],
                                                       rt["eq"][:]))
                        t_mask = vop(ph, lambda v, cs=cs: v.tensor_add(
                            mask_all[:, cs:cs + E], rt["gt"][:], rt["sel"][:]))
                        cumsum_freeE(ph, mask_all[:, cs:cs + E],
                                     mcum_all[:, cs:cs + E])

                        def pmm(t, cs=cs):
                            t.matmul(pb[5][:, 0:E], tri_f[:],
                                     mask_all[:, cs:cs + E], start=True, stop=True)
                            return t.matmul(pb[5][0:1, 32:32 + E], ones_c[:],
                                            mask_all[:, cs:cs + E],
                                            start=True, stop=True)
                        ptok = top(ph, pmm, t_mask, tri_bank_free)
                        vop(ph, lambda v, cs=cs: v.tensor_copy(
                            pref_all[:, cs:cs + E], pb[5][:, 0:E]), ptok)
                        tri_bank_free = vop(ph, lambda v, cs=cs: v.tensor_copy(
                            cnt_a[:, cs:cs + E], pb[5][0:1, 32:32 + E]))
                        if tt % 2 == 1:
                            pc.step(ph)

                    hpair_free = [None, None]
                    for hch in range(DCH):
                        for ts in range(4):
                            g2 = (hch * 4 + ts) % 2
                            b0, b1 = 2 * g2, 2 * g2 + 1

                            def hmm(t, hch=hch, ts=ts, pbk=pbk, b0=b0, b1=b1):
                                for ci in range(DCH):
                                    t.matmul(pb[b0][:],
                                             ws1[:, ci, hch * 128:(hch + 1) * 128],
                                             xnT[pbk][:, ci,
                                                      ts * 512:(ts + 1) * 512],
                                             start=(ci == 0), stop=(ci == DCH - 1))
                                inst = None
                                for ci in range(DCH):
                                    inst = t.matmul(
                                        pb[b1][:],
                                        ws3[:, ci, hch * 128:(hch + 1) * 128],
                                        xnT[pbk][:, ci, ts * 512:(ts + 1) * 512],
                                        start=(ci == 0), stop=(ci == DCH - 1))
                                return inst
                            mmtok = top(ph, hmm, ld_tok[pbk], wtok, hpair_free[g2])
                            stok = aop(ph, lambda a, b0=b0, g2=g2: a.activation(
                                silu_t[g2][:], pb[b0][:], AF.Silu), mmtok)
                            hpair_free[g2] = vop(
                                ph, lambda v, hch=hch, ts=ts, b1=b1, g2=g2:
                                v.tensor_tensor(hh[:, hch, ts * 512:(ts + 1) * 512],
                                                silu_t[g2][:], pb[b1][:], ALU.mult),
                                stok)
                    hh_done = s_v.cur()
                    zpair_free = [hpair_free[0], hpair_free[1]]
                    zt_free = [None, None]
                    for tt in range(16):
                        g2 = tt % 2
                        b0, b1 = 2 * g2, 2 * g2 + 1

                        def zmm(t, tt=tt, b0=b0, b1=b1):
                            inst = None
                            for bank, half in ((b0, 0), (b1, 1)):
                                for hc in range(DCH):
                                    inst = t.matmul(
                                        pb[bank][:],
                                        hh[:, hc, tt * 128:(tt + 1) * 128],
                                        ws2[:, hc, half * 512:(half + 1) * 512],
                                        start=(hc == 0), stop=(hc == DCH - 1))
                            return inst
                        mmtok = top(ph, zmm, hh_done, zpair_free[g2])
                        vop(ph, lambda v, b0=b0, g2=g2: v.tensor_copy(
                            zt[g2][:, 0:512], pb[b0][:]), mmtok, zt_free[g2])
                        zpair_free[g2] = vop(ph, lambda v, b1=b1, g2=g2:
                                             v.tensor_copy(zt[g2][:, 512:1024],
                                                           pb[b1][:]))
                        zt_free[g2] = qop(
                            ph, lambda q, t0=t0, tt=tt, g2=g2:
                            q.dma_start(
                                out=y_f[t0 + tt * 128:t0 + tt * 128 + 128, :],
                                in_=zt[g2][:]), zpair_free[g2], sem=sR_z[g2])
                    blk_use[pbk] = s_p.cur()

                pc.finish(ph)
                pc_done = None
                bufs = [cnt_a.rearrange("o (t e) -> o t e", e=E),
                        cnt_b.rearrange("o (t e) -> o t e", e=E)]
                cur = bufs[0]
                for li, dd in enumerate([1, 2, 4, 8, 16, 32]):
                    nxt = bufs[(li + 1) % 2]
                    vop(ph, lambda v, nxt=nxt, cur=cur, dd=dd:
                        v.tensor_copy(nxt[:, 0:dd, :], cur[:, 0:dd, :]))
                    vop(ph, lambda v, nxt=nxt, cur=cur, dd=dd:
                        v.tensor_add(nxt[:, dd:NT, :], cur[:, dd:NT, :],
                                     cur[:, 0:NT - dd, :]))
                    cur = nxt
                vop(ph, lambda v: v.memset(bufs[1][:, 0:1, :], 0.0))
                t_ex = vop(ph, lambda v: v.tensor_copy(
                    bufs[1][:, 1:NT, :], bufs[0][:, 0:NT - 1, :]))
                bc_free = None
                for q4 in range(4):
                    btok = top(ph, lambda t, q4=q4: t.matmul(
                        pb[6][:, 0:512], ones_r[:],
                        cnt_b[:, q4 * 512:(q4 + 1) * 512], start=True, stop=True),
                        t_ex, bc_free, gate_bank_free[0])
                    bc_free = vop(ph, lambda v, q4=q4: v.tensor_copy(
                        toff_all[:, q4 * 512:(q4 + 1) * 512], pb[6][:, 0:512]),
                        btok)
                toff_done = bc_free

                scat_tok = [None, None]
                sstore_tok = [None, None]
                for tile in range(NT):
                    p2 = tile % 2
                    cs = tile * E
                    xldtok = qop(ph, lambda q, tile=tile, p2=p2: q.dma_start(
                        out=xn_sb[p2][:],
                        in_=xn_bf[tile * 128:(tile + 1) * 128, :]),
                        a_store_g, scat_tok[p2], sem=sR_xs[p2])
                    vop(ph, lambda v, cs=cs: v.tensor_add(
                        rt["ca"][:], iotaEM[:], toff_all[:, cs:cs + E]),
                        toff_done, scat_tok[p2])
                    vop(ph, lambda v, cs=cs: v.tensor_add(
                        rt["cb"][:], rt["ca"][:], pref_all[:, cs:cs + E]))
                    vop(ph, lambda v: v.tensor_tensor(slotm[:], rt["cb"][:],
                                                      clampM[:], ALU.min))
                    vop(ph, lambda v, tile=tile: v.tensor_scalar_add(
                        idcol[:], iota_p[:], float(tile * 128)))
                    for j in range(4):
                        vop(ph, lambda v, j=j, p2=p2: v.tensor_copy(
                            packb[p2][:, 2 * j:2 * j + 1], idcol[:]))
                        vop(ph, lambda v, j=j, cs=cs: v.tensor_single_scalar(
                            t_oh[:], mcum_all[:, cs:cs + E], float(j + 1),
                            ALU.is_equal))
                        vop(ph, lambda v, cs=cs: v.tensor_mul(
                            t_oh[:], t_oh[:], mask_all[:, cs:cs + E]))
                        vop(ph, lambda v: v.tensor_mul(t_tmpE[:], t_oh[:],
                                                       slotm[:]))
                        vop(ph, lambda v, j=j, p2=p2: v.reduce_sum(
                            slotf[p2][:, j:j + 1], t_tmpE[:], axis=AX))
                        vop(ph, lambda v, cs=cs: v.tensor_mul(
                            t_tmpE[:], t_oh[:], sc_all[:, cs:cs + E]))
                        vop(ph, lambda v, j=j, p2=p2: v.reduce_sum(
                            packb[p2][:, 2 * j + 1:2 * j + 2], t_tmpE[:], axis=AX))
                    t_slot = vop(ph, lambda v, p2=p2: v.tensor_copy(
                        slotu[p2][:], slotf[p2][:]), sstore_tok[p2])
                    sstore_tok[p2] = qop(ph, lambda q, tile=tile, p2=p2:
                                         q.dma_start(
                                             out=slots_tok[tile * 128:
                                                           (tile + 1) * 128, :],
                                             in_=slotu[p2][:]), t_slot,
                                         sem=sR_ss[p2])
                    for j in range(4):
                        gop(ph, lambda g, j=j, p2=p2: g.indirect_dma_start(
                            out=xg,
                            out_offset=IndirectOffsetOnAxis(
                                slotu[p2][:, j:j + 1], 0),
                            in_=xn_sb[p2][:], in_offset=None),
                            t_slot, xldtok, sem=sR_sc[p2])
                        gop(ph, lambda g, j=j, p2=p2: g.indirect_dma_start(
                            out=idw,
                            out_offset=IndirectOffsetOnAxis(
                                slotu[p2][:, j:j + 1], 0),
                            in_=packb[p2][:, 2 * j:2 * j + 2], in_offset=None),
                            sem=sR_sc[p2])
                    scat_tok[p2] = sR_sc[p2].cur()
                r_scat_done = None
                r_y_done = None
                r_slots_done = None
                run(ph, "phR")

        s_rw_done = None
        c_scat_done = None
        if upto >= 4:
            with ExitStack() as c1:
                vb1 = [sbt(f"S_v{i}", [128, TSEQ], F32, c1) for i in range(2)]
                zb1 = sbt("S_z", [128, TSEQ + 2], F32, c1)
                sb1 = [sbt(f"S_s{i}", [128, TSEQ + 2], F32, c1) for i in range(2)]
                srv1 = sbt("S_srv", [128, TSEQ + 1], F32, c1)
                zrv1 = sbt("S_zrv", [128, TSEQ + 1], F32, c1)
                Af1 = sbt("S_Af", [128, TSEQ], F32, c1)
                Bf1 = sbt("S_Bf", [128, TSEQ], F32, c1)
                Nf1 = sbt("S_Nf", [128, TSEQ], F32, c1)
                Df1 = sbt("S_Df", [128, TSEQ], F32, c1)
                rb1 = [sbt(f"S_r{i}", [128, TSEQ], BF16, c1) for i in range(2)]
                rwt1 = sbt("S_rw", [128, TSEQ], BF16, c1)
                xgT = [sbt(f"C_xgT{i}", [128, DCH, M], BF16, c1) for i in range(2)]
                h3t = [sbt(f"C_h3t{i}", [128, 512], BF16, c1) for i in range(2)]
                w1sb = [sbt(f"C_W1{i}", [128, DCH, H], BF16, c1) for i in range(2)]
                w3sb = [sbt(f"C_W3{i}", [128, DCH, H], BF16, c1) for i in range(2)]
                w2sb = [sbt("C_W2x", [128, 4, D], BF16, c1)] * 2
                hhE = sbt("C_hh", [128, 4, M], BF16, c1)
                silu_e = [sbt(f"C_silu{i}", [128, 512], BF16, c1) for i in range(2)]
                outsb = [sbt(f"C_out{i}", [128, D], BF16, c1) for i in range(4)]
                idwst = [sbt(f"C_idw{i}", [128, 10, 2], F32, c1) for i in range(2)]

                ph = Ph()
                sS_k = dsem("sS_k", 2)
                sS_v = dsem("sS_v", 2)
                sS_r = dsem("sS_r", 2)
                sS_rw = dsem("sS_rw")
                sC_x = dsem("sC_x", 2)
                sC_w = dsem("sC_w", 2)
                sC_w2 = dsem("sC_w2")
                sC_i = dsem("sC_i", 2)
                sC_o = dsem("sC_o", 4)
                r_tail = [s_v.cur(), s_a.cur()]
                for buf in (zb1, sb1[0], sb1[1]):
                    vop(ph, lambda v, buf=buf: v.memset(buf[:, 0:1], 0.0))
                    vop(ph, lambda v, buf=buf: v.memset(
                        buf[:, TSEQ + 1:TSEQ + 2], 0.0))
                for buf in (srv1, zrv1):
                    vop(ph, lambda v, buf=buf: v.memset(buf[:, 0:1], 0.0))

                sb_free = [None, None]
                vb_free = [None, None]
                rb_free = [None, None]
                zb_free = [None]
                srw_st = None

                def sched_scan(ph, u):
                    nonlocal srw_st
                    p = u % 2
                    cch, sq = divmod(u, NSEQ)
                    c0 = sq * TSEQ
                    dbc = d_ex[:, cch:cch + 1].to_broadcast([128, TSEQ])
                    euc = eu[:, cch:cch + 1]
                    lk = qop(ph, lambda a, cch=cch, c0=c0, p=p: a.dma_start(
                        out=sb1[p][:, 1:TSEQ + 1],
                        in_=k_f[cch, :, c0:c0 + TSEQ]), b_store, sb_free[p])
                    lv = qop(ph, lambda a, cch=cch, c0=c0, p=p: a.dma_start(
                        out=vb1[p][:], in_=v_f[cch, :, c0:c0 + TSEQ]), vb_free[p])
                    lr = qop(ph, lambda a, cch=cch, c0=c0, p=p: a.dma_start(
                        out=rb1[p][:], in_=r_f[cch, :, c0:c0 + TSEQ]), rb_free[p])
                    ek = aop(ph, lambda a, p=p: a.activation(
                        zb1[:, 1:TSEQ + 1], sb1[p][:, 1:TSEQ + 1], AF.Exp),
                        lk, zb_free[0])
                    vop(ph, lambda v, p=p: v.tensor_tensor(
                        sb1[p][:, 1:TSEQ + 1], zb1[:, 1:TSEQ + 1], vb1[p][:],
                        ALU.mult), ek, lv)
                    vop(ph, lambda v, p=p: v.tensor_copy(
                        srv1[:, 1:TSEQ + 1], sb1[p][:, TSEQ:0:-1]))
                    vop(ph, lambda v: v.tensor_copy(
                        zrv1[:, 1:TSEQ + 1], zb1[:, TSEQ:0:-1]))
                    for dst, d1 in ((Af1, sb1[p]), (Bf1, zb1)):
                        vop(ph, lambda v, dst=dst, d1=d1, dbc=dbc:
                            v.tensor_tensor_scan(dst[:], dbc, d1[:, 0:TSEQ],
                                                 0.0, ALU.mult, ALU.add))
                        emit(ph, "vector", lambda v: v.engine_nop())
                    sb_free[p] = vop(ph, lambda v, euc=euc, p=p:
                                     v.scalar_tensor_tensor(
                                         Nf1[:], sb1[p][:, 1:TSEQ + 1], euc,
                                         Af1[:], ALU.mult, ALU.add))
                    zb_free[0] = vop(ph, lambda v, euc=euc:
                                     v.scalar_tensor_tensor(
                                         Df1[:], zb1[:, 1:TSEQ + 1], euc,
                                         Bf1[:], ALU.mult, ALU.add))
                    for dst, d1 in ((Af1, srv1), (Bf1, zrv1)):
                        vop(ph, lambda v, dst=dst, d1=d1, dbc=dbc:
                            v.tensor_tensor_scan(dst[:], dbc, d1[:, 0:TSEQ],
                                                 0.0, ALU.mult, ALU.add))
                        emit(ph, "vector", lambda v: v.engine_nop())
                    vop(ph, lambda v, euc=euc, p=p: v.scalar_tensor_tensor(
                        vb1[p][:], srv1[:, 1:TSEQ + 1], euc, Af1[:],
                        ALU.mult, ALU.add))
                    t_d2 = vop(ph, lambda v, euc=euc: v.scalar_tensor_tensor(
                        Af1[:], zrv1[:, 1:TSEQ + 1], euc, Bf1[:],
                        ALU.mult, ALU.add))
                    aop(ph, lambda a: a.activation(Bf1[:], Df1[:], AF.Ln), t_d2)
                    aop(ph, lambda a: a.activation(
                        Df1[:], Bf1[:], AF.Exp, scale=-1.0, bias=lnh_c[:]))
                    aop(ph, lambda a: a.activation(Bf1[:], Af1[:], AF.Ln))
                    t_rec = aop(ph, lambda a: a.activation(
                        Af1[:], Bf1[:], AF.Exp, scale=-1.0, bias=lnh_c[:]))
                    vop(ph, lambda v: v.tensor_mul(
                        srv1[:, 1:TSEQ + 1], Nf1[:], Df1[:]), t_rec, lr)
                    vb_free[p] = vop(ph, lambda v, p=p: v.tensor_mul(
                        Nf1[:], vb1[p][:], Af1[:]))
                    vop(ph, lambda v: v.tensor_add(
                        Df1[:], srv1[:, 1:TSEQ + 1], Nf1[:, ::-1]))
                    rb_free[p] = vop(ph, lambda v, p=p: v.tensor_tensor(
                        rwt1[:], Df1[:], rb1[p][:], ALU.mult), srw_st)
                    srw_st = sqop(ph, lambda a, cch=cch, c0=c0: a.dma_start(
                        out=rw_bf[cch, :, c0:c0 + TSEQ], in_=rwt1[:]),
                        rb_free[p], sem=sS_rw)

                in_use = [None, None]
                hh_rd = None
                out_free = [None] * 4
                prev_wts_use = [None, None]
                hfree = [None, None]
                wfree = [None, None]

                def sched_expert(ph, e):
                    nonlocal hh_rd
                    p = e % 2
                    ldi = qop(ph, lambda q, e=e, p=p: q.dma_start(
                        out=idwst[p][:],
                        in_=idw[e * M:(e + 1) * M, :].rearrange(
                            "(a p) w -> p a w", p=128)),
                        in_use[p], prev_wts_use[p], *r_tail, sem=sC_i[p])
                    lds = None
                    for dch in range(DCH):
                        lds = qop(ph, lambda q, e=e, p=p, dch=dch:
                                  q.dma_start_transpose(
                                      xgT[p][:, dch, :],
                                      xg[e * M:(e + 1) * M,
                                         dch * 128:(dch + 1) * 128]),
                                  in_use[p], sem=sC_x[p])
                    ldw = None
                    for dst, srcw in [(w1sb, w1_bf), (w3sb, w3_bf)]:
                        ldw = qop(ph, lambda q, dst=dst, srcw=srcw, e=e, p=p:
                                  q.dma_start(out=dst[p][:], in_=srcw[e]),
                                  in_use[p], sem=sC_w[p])
                    ldw2 = qop(ph, lambda q, e=e, p=p:
                               q.dma_start(out=w2sb[p][:], in_=w2_bf[e]),
                               hh_rd, sem=sC_w2)
                    for hs in range(4):
                        for rt_ in range(3):
                            r0 = rt_ * 512
                            rw_ = 512 if rt_ < 2 else M - 1024
                            g2 = (hs * 3 + rt_) % 2
                            b0, b1 = 2 * g2, 2 * g2 + 1

                            def wmm(t, hs=hs, r0=r0, rw_=rw_, p=p, b0=b0, b1=b1):
                                for ci in range(DCH):
                                    t.matmul(pb[b0][:, 0:rw_],
                                             w1sb[p][:, ci,
                                                     hs * 128:(hs + 1) * 128],
                                             xgT[p][:, ci, r0:r0 + rw_],
                                             start=(ci == 0),
                                             stop=(ci == DCH - 1))
                                inst = None
                                for ci in range(DCH):
                                    inst = t.matmul(
                                        pb[b1][:, 0:rw_],
                                        w3sb[p][:, ci, hs * 128:(hs + 1) * 128],
                                        xgT[p][:, ci, r0:r0 + rw_],
                                        start=(ci == 0), stop=(ci == DCH - 1))
                                return inst
                            mmtok = top(ph, wmm, lds, ldw, hfree[g2])
                            aop(ph, lambda a, b0=b0, rw_=rw_, g2=g2:
                                a.activation(silu_e[g2][:, 0:rw_],
                                             pb[b0][:, 0:rw_], AF.Silu),
                                mmtok, hfree[g2])
                            stok = aop(ph, lambda a, b1=b1, rw_=rw_, g2=g2:
                                       a.activation(h3t[g2][:, 0:rw_],
                                                    pb[b1][:, 0:rw_], AF.Copy),
                                       hfree[g2])
                            hfree[g2] = kop(
                                ph, lambda g, hs=hs, r0=r0, rw_=rw_, g2=g2:
                                g.tensor_tensor(hhE[:, hs, r0:r0 + rw_],
                                                silu_e[g2][:, 0:rw_],
                                                h3t[g2][:, 0:rw_], ALU.mult),
                                stok, hh_rd)
                    in_use[p] = s_p.cur()
                    hh_done = s_g.cur()
                    for rti in range(M // 128):
                        ob = rti % 4
                        g2 = rti % 2
                        b0, b1 = 4 + 2 * g2, 5 + 2 * g2

                        def w2mm(t, rti=rti, p=p, b0=b0, b1=b1):
                            inst = None
                            for bank, half in ((b0, 0), (b1, 1)):
                                for hs in range(4):
                                    inst = t.matmul(
                                        pb[bank][:],
                                        hhE[:, hs, rti * 128:(rti + 1) * 128],
                                        w2sb[p][:, hs,
                                                half * 512:(half + 1) * 512],
                                        start=(hs == 0), stop=(hs == 3))
                            return inst
                        mmtok = top(ph, w2mm, hh_done, wfree[g2], ldw2)
                        aop(ph, lambda a, rti=rti, ob=ob, b0=b0, p=p:
                            a.activation(outsb[ob][:, 0:512], pb[b0][:],
                                         AF.Copy, scale=idwst[p][:, rti, 1:2]),
                            mmtok, out_free[ob])
                        wfree[g2] = aop(
                            ph, lambda a, rti=rti, ob=ob, b1=b1, p=p:
                            a.activation(outsb[ob][:, 512:1024], pb[b1][:],
                                         AF.Copy, scale=idwst[p][:, rti, 1:2]))
                        evtok = s_a.cur()
                        out_free[ob] = sqop(
                            ph, lambda q, rti=rti, ob=ob, e=e:
                            q.dma_start(
                                out=out_slots[e * M + rti * 128:
                                              e * M + (rti + 1) * 128, :],
                                in_=outsb[ob][:]), evtok, sem=sC_o[ob])
                    hh_rd = s_p.cur()
                    prev_wts_use[p] = s_a.cur()

                for i in range(E):
                    sched_expert(ph, i)
                    sched_scan(ph, i)
                s_rw_done = None
                c_scat_done = None
                run(ph, "phSC")

        if upto >= 6:
            with ExitStack() as c1:
                w2bc = sbt("w2bc", [128, D], F32, c1)
                b2bc = sbt("b2bc", [128, D], F32, c1)
                rwT = [sbt(f"D_rwT{i}", [128, DCH, TSEQ], BF16, c1)
                       for i in range(2)]
                woT = sbt("D_woT", [128, DCH, D], BF16, c1)
                xnt = [sbt(f"D_xn{i}", [128, D], F32, c1) for i in range(2)]
                xrt = [sbt(f"D_xr{i}", [128, D], F32, c1) for i in range(2)]
                yrt = [sbt(f"D_yr{i}", [128, D], F32, c1) for i in range(2)]
                a2t = [sbt(f"D_a2{i}", [128, D], F32, c1) for i in range(2)]
                axy = [sbt(f"D_axy{i}", [128, D], F32, c1) for i in range(2)]
                sqd = [sbt(f"D_sq{i}", [128, D], F32, c1) for i in range(2)]
                t1d = [sbt(f"D_t1{i}", [128, D], F32, c1) for i in range(2)]
                xn2t = [sbt(f"D_xn2{i}", [128, D], BF16, c1) for i in range(2)]
                st5 = [[sbt(f"D_s{j}_{i}", [128, 1], F32, c1) for j in range(5)]
                       for i in range(2)]
                ph = Ph()
                sD_r = dsem("sD_r", 2)
                sD_w = dsem("sD_w")
                sD_x = dsem("sD_x", 2)
                sD_xr = dsem("sD_xr", 2)
                sD_y = dsem("sD_y", 2)
                sD_s1 = dsem("sD_s1", 2)
                sD_s2 = dsem("sD_s2", 2)
                wtok = gop(ph, lambda g: g.dma_start(
                    out=woT[:], in_=WoT.rearrange("(c p) o -> p c o", p=128)),
                    sem=sD_w)
                ln2_tok = qop(ph, lambda q: q.dma_start(
                    out=w2bc[:], in_=ln2w.partition_broadcast(128)), sem=sD_w)
                ln2_tok = qop(ph, lambda q: q.dma_start(
                    out=b2bc[:], in_=ln2b.partition_broadcast(128)), sem=sD_w)
                blk_use = [None, None]
                cons_xn = [None, None]
                cons_a2 = [None, None]
                cons_axy = [None, None]
                cons_x2 = [None, None]
                d1sq = [None, None]
                sc_tail = [s_a.cur(), s_g.cur(), s_v.cur()]
                for tb in range(NB):
                    pbk = tb % 2
                    t0 = tb * TSEQ
                    ldr = None
                    for cch in range(DCH):
                        ldr = qop(ph, lambda q, pbk=pbk, cch=cch, t0=t0:
                                  q.dma_start(
                                      out=rwT[pbk][:, cch, :],
                                      in_=rw_bf[cch, :, t0:t0 + TSEQ]),
                                  blk_use[pbk], *sc_tail, sem=sD_r[pbk])
                    for tt in range(16):
                        p = tt % 2
                        r0 = t0 + tt * 128
                        b0, b1 = 2 * p, 2 * p + 1
                        ldx = qop(ph, lambda q, p=p, r0=r0: q.dma_start(
                            out=xnt[p][:], in_=xn_f[r0:r0 + 128, :]), cons_xn[p],
                            sem=sD_x[p])
                        ldxr = qop(ph, lambda q, p=p, r0=r0: q.dma_start(
                            out=xrt[p][:], in_=x_in[r0:r0 + 128, :]),
                            cons_axy[p], sem=sD_xr[p])
                        ldy = qop(ph, lambda q, p=p, r0=r0: q.dma_start(
                            out=yrt[p][:], in_=y_f[r0:r0 + 128, :]),
                            cons_axy[p], sem=sD_y[p])

                        def omm(t, tt=tt, pbk=pbk, b0=b0, b1=b1):
                            inst = None
                            for bank, half in ((b0, 0), (b1, 1)):
                                for cch in range(DCH):
                                    inst = t.matmul(
                                        pb[bank][:],
                                        rwT[pbk][:, cch, tt * 128:(tt + 1) * 128],
                                        woT[:, cch, half * 512:(half + 1) * 512],
                                        start=(cch == 0), stop=(cch == DCH - 1))
                            return inst
                        mmtok = top(ph, omm, ldr, wtok, cons_a2[p],
                                    *sc_tail)
                        vop(ph, lambda v, p=p, b0=b0: v.tensor_add(
                            a2t[p][:, 0:512], xnt[p][:, 0:512], pb[b0][:]),
                            mmtok, ldx)
                        t_a2 = vop(ph, lambda v, p=p, b1=b1: v.tensor_add(
                            a2t[p][:, 512:1024], xnt[p][:, 512:1024], pb[b1][:]))
                        vop(ph, lambda v, p=p: v.reduce_sum(
                            st5[p][0][:], a2t[p][:], axis=AX))
                        t_nm = vop(ph, lambda v, p=p: v.tensor_scalar_mul(
                            st5[p][1][:], st5[p][0][:], -1.0 / D), d1sq[p])
                        t_sq = aop(ph, lambda a, p=p: a.activation(
                            sqd[p][:], a2t[p][:], AF.Square, bias=st5[p][1][:],
                            accum_out=st5[p][2][:]), t_nm)
                        d1sq[p] = t_sq
                        aop(ph, lambda a, p=p: a.activation(
                            st5[p][3][:], st5[p][2][:], AF.Ln, bias=eps_c[:],
                            scale=1.0 / D))
                        t_rs = aop(ph, lambda a, p=p: a.activation(
                            st5[p][4][:], st5[p][3][:], AF.Exp, scale=-0.5))
                        vop(ph, lambda v, p=p: v.tensor_scalar(
                            t1d[p][:], a2t[p][:], st5[p][1][:], st5[p][4][:],
                            ALU.add, ALU.mult), t_rs)
                        vop(ph, lambda v, p=p: v.tensor_mul(
                            t1d[p][:], t1d[p][:], w2bc[:]), ln2_tok)
                        t_x2 = vop(ph, lambda v, p=p: v.tensor_tensor(
                            xn2t[p][:], t1d[p][:], b2bc[:], ALU.add),
                            cons_x2[p])
                        vop(ph, lambda v, p=p: v.tensor_add(
                            axy[p][:], a2t[p][:], xrt[p][:]), ldxr, ldy,
                            cons_a2[p])
                        t_axy = vop(ph, lambda v, p=p: v.tensor_add(
                            axy[p][:], axy[p][:], yrt[p][:]))
                        cons_axy[p] = t_axy
                        st1 = qop(ph, lambda q, p=p, r0=r0: q.dma_start(
                            out=a2xy_f[r0:r0 + 128, :], in_=axy[p][:]), t_axy,
                            sem=sD_s1[p])
                        st2 = qop(ph, lambda q, p=p, r0=r0: q.dma_start(
                            out=xn2_bf[r0:r0 + 128, :], in_=xn2t[p][:]), t_x2,
                            sem=sD_s2[p])
                        cons_a2[p] = st1
                        cons_xn[p] = st2
                        cons_x2[p] = st2
                    blk_use[pbk] = s_p.cur()
                d1_store = None
                run(ph, "phD1")

            with ExitStack() as c1:
                wfk = sbt("D_wfk", [128, DCH, HK], BF16, c1)
                xn2T = [sbt(f"D_x2T{i}", [128, DCH, TSEQ], BF16, c1)
                        for i in range(2)]
                relu_t = [sbt(f"D_rel{i}", [128, 512], F32, c1) for i in range(3)]
                kst = [sbt(f"D_kst{i}", [128, 512], BF16, c1) for i in range(3)]
                ph = Ph()
                s2_T = dsem("s2_T", 2)
                s2_s = dsem("s2_s", 3)
                s2_w = dsem("s2_w")
                wtok = gop(ph, lambda g: g.dma_start(
                    out=wfk[:], in_=WfkT.rearrange("(c p) o -> p c o", p=128)),
                    sem=s2_w)
                blk_use = [None, None]
                bank_free = [None] * 4
                buf_free = [None] * 3
                for tb in range(NB):
                    pbk = tb % 2
                    t0 = tb * TSEQ
                    ldt = None
                    for dch in range(DCH):
                        ldt = qop(ph, lambda q, pbk=pbk, dch=dch, t0=t0:
                                  q.dma_start_transpose(
                                      xn2T[pbk][:, dch, :],
                                      xn2_bf[t0:t0 + TSEQ,
                                             dch * 128:(dch + 1) * 128]),
                                  blk_use[pbk], sem=s2_T[pbk])
                    gi = 0
                    for hkc in range(HKCH):
                        for ts in range(4):
                            bank = gi % 4
                            bi = gi % 3
                            gi += 1

                            def kmm(t, hkc=hkc, ts=ts, pbk=pbk, bank=bank):
                                inst = None
                                for ci in range(DCH):
                                    inst = t.matmul(
                                        pb[bank][:],
                                        wfk[:, ci, hkc * 128:(hkc + 1) * 128],
                                        xn2T[pbk][:, ci, ts * 512:(ts + 1) * 512],
                                        start=(ci == 0), stop=(ci == DCH - 1))
                                return inst
                            mmtok = top(ph, kmm, ldt, wtok, bank_free[bank])
                            bank_free[bank] = vop(
                                ph, lambda v, bank=bank, bi=bi:
                                v.tensor_scalar_max(relu_t[bi][:], pb[bank][:],
                                                    0.0),
                                mmtok, buf_free[bi])
                            t_sqr = aop(ph, lambda a, bi=bi: a.activation(
                                kst[bi][:], relu_t[bi][:], AF.Square),
                                bank_free[bank])
                            buf_free[bi] = qop(
                                ph, lambda q, hkc=hkc, ts=ts, t0=t0, bi=bi:
                                q.dma_start(
                                    out=kk_bf[hkc, :,
                                              t0 + ts * 512:t0 + ts * 512 + 512],
                                    in_=kst[bi][:]), t_sqr, sem=s2_s[bi])
                    blk_use[pbk] = s_p.cur()
                d2a_store = None
                run(ph, "phD2a")

            with ExitStack() as c1:
                wfv = sbt("D_wfv", [128, HKCH, D], BF16, c1)
                wfr = sbt("D_wfr", [128, DCH, D], BF16, c1)
                kks = [sbt(f"D_kks{i}", [128, HKCH, 512], BF16, c1)
                       for i in range(2)]
                xn2Th = [sbt(f"D_x2h{i}", [128, DCH, 512], BF16, c1)
                         for i in range(2)]
                a2l = [sbt(f"D_a2l{i}", [128, D], F32, c1) for i in range(2)]
                sg = [sbt(f"D_sg{i}", [128, D], F32, c1) for i in range(2)]
                ot = [sbt(f"D_ot{i}", [128, D], F32, c1) for i in range(2)]
                gbuf = [sbt(f"D_gb{i}", [128, D], BF16, c1) for i in range(8)]
                stl = [sbt(f"D_stl{i}", [128, 4], U32, c1) for i in range(2)]
                ph = Ph()
                s3_k = dsem("s3_k", 2)
                s3_T = dsem("s3_T", 2)
                s3_a = dsem("s3_a", 2)
                s3_s = dsem("s3_s", 2)
                s3_g = dsem("s3_g", 2)
                s3_o = dsem("s3_o", 2)
                s3_w = dsem("s3_w")
                wtok = gop(ph, lambda g: g.dma_start(
                    out=wfv[:], in_=WfvT.rearrange("(c p) o -> p c o", p=128)),
                    sem=s3_w)
                wtok = gop(ph, lambda g: g.dma_start(
                    out=wfr[:], in_=WfrT.rearrange("(c p) o -> p c o", p=128)),
                    sem=s3_w)
                blk_use = [None, None]
                cons = [None, None]
                out_st = [None, None]
                gts_tok = [None, None]
                for hb in range(16):
                    pbk = hb % 2
                    h0 = hb * 512
                    ldk = None
                    for hkc in range(HKCH):
                        ldk = qop(ph, lambda q, pbk=pbk, hkc=hkc, h0=h0:
                                  q.dma_start(
                                      out=kks[pbk][:, hkc, :],
                                      in_=kk_bf[hkc, :, h0:h0 + 512]),
                                  blk_use[pbk], sem=s3_k[pbk])
                    ldt = None
                    for dch in range(DCH):
                        ldt = qop(ph, lambda q, pbk=pbk, dch=dch, h0=h0:
                                  q.dma_start_transpose(
                                      xn2Th[pbk][:, dch, :],
                                      xn2_bf[h0:h0 + 512,
                                             dch * 128:(dch + 1) * 128]),
                                  blk_use[pbk], sem=s3_T[pbk])
                    for tt in range(4):
                        p = tt % 2
                        r0 = h0 + tt * 128
                        fb0, fb1 = 2 * p, 2 * p + 1
                        rb0, rb1 = 4 + 2 * p, 5 + 2 * p
                        ldx = qop(ph, lambda q, p=p, r0=r0: q.dma_start(
                            out=a2l[p][:], in_=a2xy_f[r0:r0 + 128, :]), cons[p],
                            sem=s3_a[p])
                        lds = qop(ph, lambda q, p=p, r0=r0: q.dma_start(
                            out=stl[p][:], in_=slots_tok[r0:r0 + 128, :]),
                            cons[p], gts_tok[p], sem=s3_s[p])
                        for j in range(4):
                            gop(ph, lambda g, p=p, j=j:
                                g.indirect_dma_start(
                                    out=gbuf[4 * p + j][:], out_offset=None,
                                    in_=out_slots,
                                    in_offset=IndirectOffsetOnAxis(
                                        stl[p][:, j:j + 1], 0)),
                                lds, cons[p], sem=s3_g[p])
                        gts_tok[p] = s3_g[p].cur()

                        def fmm(t, tt=tt, pbk=pbk, fb0=fb0, fb1=fb1,
                                rb0=rb0, rb1=rb1):
                            for bank, half in ((fb0, 0), (fb1, 1)):
                                for hkc in range(HKCH):
                                    t.matmul(
                                        pb[bank][:],
                                        kks[pbk][:, hkc,
                                                 tt * 128:(tt + 1) * 128],
                                        wfv[:, hkc, half * 512:(half + 1) * 512],
                                        start=(hkc == 0),
                                        stop=(hkc == HKCH - 1))
                            inst = None
                            for bank, half in ((rb0, 0), (rb1, 1)):
                                for ci in range(DCH):
                                    inst = t.matmul(
                                        pb[bank][:],
                                        xn2Th[pbk][:, ci,
                                                   tt * 128:(tt + 1) * 128],
                                        wfr[:, ci, half * 512:(half + 1) * 512],
                                        start=(ci == 0), stop=(ci == DCH - 1))
                            return inst
                        mmtok = top(ph, fmm, ldk, ldt, wtok, cons[p])
                        aop(ph, lambda a, p=p, rb0=rb0: a.activation(
                            sg[p][:, 0:512], pb[rb0][:], AF.Sigmoid), mmtok)
                        t_sg = aop(ph, lambda a, p=p, rb1=rb1: a.activation(
                            sg[p][:, 512:1024], pb[rb1][:], AF.Sigmoid))
                        vop(ph, lambda v, p=p, fb0=fb0: v.tensor_tensor(
                            ot[p][:, 0:512], sg[p][:, 0:512], pb[fb0][:],
                            ALU.mult), t_sg, mmtok, out_st[p])
                        vop(ph, lambda v, p=p, fb1=fb1: v.tensor_tensor(
                            ot[p][:, 512:1024], sg[p][:, 512:1024], pb[fb1][:],
                            ALU.mult))
                        vop(ph, lambda v, p=p: v.tensor_add(
                            ot[p][:], ot[p][:], a2l[p][:]), ldx)
                        for j in range(4):
                            vop(ph, lambda v, p=p, j=j: v.tensor_add(
                                ot[p][:], ot[p][:], gbuf[4 * p + j][:]),
                                gts_tok[p])
                        t_o = s_v.cur()
                        cons[p] = s_v.cur()
                        out_st[p] = qop(ph, lambda q, p=p, r0=r0: q.dma_start(
                            out=out_d[r0:r0 + 128, :], in_=ot[p][:]), t_o,
                            sem=s3_o[p])
                    blk_use[pbk] = s_p.cur()
                run(ph, "phD2b")

        ph = Ph()
        emit(ph, "sync", lambda q: q.nop(), waits=allcur())
        run(ph, "fin")

    return nc


_cache = {}


def _get_prog(upto=99, debug=False):
    key = (upto, debug)
    if key not in _cache:
        _cache[key] = build(upto=upto, debug=debug)
    return _cache[key]


def _prep_inputs(inputs):
    f = np.ascontiguousarray

    def g(name):
        return np.asarray(inputs[name])

    x = g("L0_attn_x").reshape(32, 2048, 1024)
    base = {
        "ln1w": f(g("L0_attn_ln1_w").reshape(1, D)),
        "ln1b": f(g("L0_attn_ln1_b").reshape(1, D)),
        "ln2w": f(g("L0_attn_ln2_w").reshape(1, D)),
        "ln2b": f(g("L0_attn_ln2_b").reshape(1, D)),
        "WrT": f(g("L0_attn_Wr").T),
        "WkT": f(g("L0_attn_Wk").T),
        "WvT": f(g("L0_attn_Wv").T),
        "WoT": f(g("L0_attn_Wo").T),
        "WfkT": f(g("L0_attn_Wfk").T),
        "WfvT": f(g("L0_attn_Wfv").T),
        "WfrT": f(g("L0_attn_Wfr").T),
        "Wg": f(g("L0_ffn_Wg")),
        "Ws1": f(g("L0_ffn_Ws1")),
        "Ws3": f(g("L0_ffn_Ws3")),
        "Ws2": f(g("L0_ffn_Ws2")),
        "W1": f(g("L0_ffn_W1")),
        "W3": f(g("L0_ffn_W3")),
        "W2": f(g("L0_ffn_W2")),
        "decay": f(g("L0_attn_decay").reshape(DCH, 128).T),
        "u": f(g("L0_attn_u").reshape(DCH, 128).T),
        "tri": np.triu(np.ones((128, 128), np.float32), 1),
        "onesc": np.ones((128, 1), np.float32),
        "onesr": np.ones((1, 128), np.float32),
        "iotaEM_i": np.broadcast_to((np.arange(E) * M).astype(np.float32),
                                    (128, E)).copy(),
        "iotap_i": np.arange(128, dtype=np.float32).reshape(128, 1),
    }
    maps = []
    for c in range(NCORES):
        m = dict(base)
        m["x"] = f(x[c * NSEQ:(c + 1) * NSEQ].reshape(TC, D))
        maps.append(m)
    return maps


def kernel(**inputs):
    nc = _get_prog()
    in_maps = _prep_inputs(inputs)
    res = run_bass_kernel_spmd(nc, in_maps, core_ids=list(range(NCORES)))
    out = np.stack([np.asarray(r["out"]).reshape(NSEQ, 2048, D)
                    for r in res.results])
    return out.reshape(32, 2048, D)
